# version 10
# baseline (speedup 1.0000x reference)
import numpy as np
import ml_dtypes
import concourse.bass as bass
import concourse.mybir as mybir
from concourse.bass_utils import run_bass_kernel_spmd

F32 = mybir.dt.float32
BF16 = mybir.dt.bfloat16
AF = mybir.ActivationFunctionType
ALU = mybir.AluOpType
AX = mybir.AxisListType

H = 1024
I = 2048
E_R = 8
PROJ = 256
N_LT = 32
N_Q = N_LT // 8
N_CORES = 8
HC = H // 128
IC_H = 8
LN_EPS = 1e-5

ENGINES = ("tensor", "vector", "scalar", "sync", "gpsimd")


PARANOID_ALL = [False]


class Prog:

    N_DMA_POOL = 6

    def __init__(self, nc):
        self.nc = nc
        self.ops = {e: [] for e in ENGINES}
        self.sem_names = ["pe", "dve", "act", "cc"]
        self.dma_pools = {}
        for q in ("sp", "aq", "pq"):
            pool = [f"{q}{i}" for i in range(self.N_DMA_POOL)]
            self.dma_pools[q] = pool
            self.sem_names += pool
        self.dma_rr = {q: 0 for q in self.dma_pools}
        self.counts = {s: 0 for s in self.sem_names}
        self.bufs = {}
        self.last_wait = {}
        self.eng_sem = {"tensor": "pe", "vector": "dve", "scalar": "act",
                        "sync": None, "gpsimd": None}

    def _dep(self, key):
        return self.bufs.setdefault(key, {"w": {}, "r": {}})

    def alias(self, dst, srcs):
        d = self._dep(dst)
        for sk in srcs:
            e = self._dep(sk)
            for s, v in e["w"].items():
                d["w"][s] = max(d["w"].get(s, 0), v)
            for s, v in e["r"].items():
                d["r"][s] = max(d["r"].get(s, 0), v)

    def op(self, eng, fn, reads=(), writes=(), sem=None, inc=0,
           extra_waits=(), no_self_sync=False):
        waits = {}
        for k in reads:
            for s, v in self._dep(k)["w"].items():
                waits[s] = max(waits.get(s, 0), v)
        for k in writes:
            d = self._dep(k)
            for s, v in d["w"].items():
                waits[s] = max(waits.get(s, 0), v)
            for s, v in d["r"].items():
                waits[s] = max(waits.get(s, 0), v)
        for s, v in extra_waits:
            waits[s] = max(waits.get(s, 0), v)
        own = self.eng_sem[eng]
        if (no_self_sync or eng == "tensor") and own in waits:
            del waits[own]
        emit = []
        for s, v in sorted(waits.items()):
            lw = self.last_wait.get((eng, s), 0)
            if v > lw:
                emit.append((s, v))
                self.last_wait[(eng, s)] = v
        if inc:
            assert sem is not None
            self.counts[sem] += inc
            val = self.counts[sem]
        else:
            sem, val = None, None
        self.ops[eng].append((emit, fn, sem, inc))
        if inc:
            for k in reads:
                d = self._dep(k)
                d["r"][sem] = max(d["r"].get(sem, 0), val)
            for k in writes:
                d = self._dep(k)
                d["w"] = {sem: val}
                d["r"] = {}
        else:
            assert not writes, f"write without inc: {writes}"
        if PARANOID_ALL[0] and inc:
            self.barrier()
        return val

    def dma_group(self, eng, queue, items):
        pool = self.dma_pools[queue]
        sem = pool[self.dma_rr[queue] % len(pool)]
        self.dma_rr[queue] += 1
        prior_end = self.counts[sem]
        if prior_end > self.last_wait.get((eng, sem), 0):
            self.ops[eng].append(([(sem, prior_end)], None, None, 0))
            self.last_wait[(eng, sem)] = prior_end
        group_end = prior_end + 16 * len(items)
        item_waits = []
        for fn, reads, writes in items:
            waits = {}
            for k in reads:
                for s, v in self._dep(k)["w"].items():
                    waits[s] = max(waits.get(s, 0), v)
            for k in writes:
                d = self._dep(k)
                for s, v in d["w"].items():
                    waits[s] = max(waits.get(s, 0), v)
                for s, v in d["r"].items():
                    waits[s] = max(waits.get(s, 0), v)
            item_waits.append(waits)
        for (fn, reads, writes), waits in zip(items, item_waits):
            emit = []
            for s, v in sorted(waits.items()):
                lw = self.last_wait.get((eng, s), 0)
                if v > lw:
                    emit.append((s, v))
                    self.last_wait[(eng, s)] = v
            self.counts[sem] += 16
            self.ops[eng].append((emit, fn, sem, 16))
        for fn, reads, writes in items:
            for k in reads:
                d = self._dep(k)
                d["r"][sem] = max(d["r"].get(sem, 0), group_end)
            for k in writes:
                d = self._dep(k)
                d["w"] = {sem: group_end}
                d["r"] = {}
        assert self.counts[sem] == group_end
        if PARANOID_ALL[0]:
            self.barrier()
        return group_end

    def barrier(self):
        snap = [(s, v) for s, v in self.counts.items() if v > 0]
        for eng in ENGINES:
            emit = []
            for s, v in snap:
                if v > self.last_wait.get((eng, s), 0):
                    emit.append((s, v))
                    self.last_wait[(eng, s)] = v
            if emit:
                self.ops[eng].append((emit, None, None, 0))

    def waitpoint(self, eng, reads):
        waits = {}
        for k in reads:
            for s, v in self._dep(k)["w"].items():
                waits[s] = max(waits.get(s, 0), v)
        emit = []
        for s, v in sorted(waits.items()):
            lw = self.last_wait.get((eng, s), 0)
            if v > lw:
                emit.append((s, v))
                self.last_wait[(eng, s)] = v
        if emit:
            self.ops[eng].append((emit, None, None, 0))

    def replay(self, block, sems):
        def make_body(eng_name):
            def body(e):
                for waits, fn, sem, inc in self.ops[eng_name]:
                    for s, v in waits:
                        e.wait_ge(sems[s], v)
                    if fn is not None:
                        ins = fn(e)
                        if inc:
                            ins.then_inc(sems[sem], inc)
            return body
        block.tensor(make_body("tensor"))
        block.vector(make_body("vector"))
        block.scalar(make_body("scalar"))
        block.sync(make_body("sync"))
        block.gpsimd(make_body("gpsimd"))


def build_nc(n_lt=N_LT, n_routed=E_R, stages="faf", attn_debug=False, paranoid=""):
    paranoid = set(str(paranoid).split(",")) if paranoid else set()
    PARANOID_ALL[0] = ("all" in paranoid)

    class _Sect:
        def __init__(self, name):
            self.name = name
        def __enter__(self):
            self.prev = PARANOID_ALL[0]
            if self.name in paranoid:
                PARANOID_ALL[0] = True
        def __exit__(self, *a):
            PARANOID_ALL[0] = self.prev
            return False
    nc = bass.Bass("TRN2", target_bir_lowering=False,
                   dynamic_dma_scratch_size=512)
    NT = n_lt * 128
    n_q = n_lt // 8

    x_in = nc.dram_tensor("x_in", [NT, H], F32, kind="ExternalInput").ap()
    wg_s = nc.dram_tensor("wg_s", [I, H], F32, kind="ExternalInput").ap()
    wu_s = nc.dram_tensor("wu_s", [I, H], F32, kind="ExternalInput").ap()
    wd_s = nc.dram_tensor("wd_s", [H, I], F32, kind="ExternalInput").ap()
    wg_r = nc.dram_tensor("wg_r", [n_routed, I, H], F32, kind="ExternalInput").ap()
    wu_r = nc.dram_tensor("wu_r", [n_routed, I, H], F32, kind="ExternalInput").ap()
    wd_r = nc.dram_tensor("wd_r", [n_routed, H, I], F32, kind="ExternalInput").ap()
    wr_t = nc.dram_tensor("wr_t", [H, n_routed], F32, kind="ExternalInput").ap()
    rb_b = nc.dram_tensor("rb_b", [128, n_routed], F32, kind="ExternalInput").ap()
    qproj = nc.dram_tensor("qproj", [H, PROJ], F32, kind="ExternalInput").ap()
    kproj = nc.dram_tensor("kproj", [H, PROJ], F32, kind="ExternalInput").ap()
    x_out = nc.dram_tensor("x_out", [NT, H], F32, kind="ExternalOutput").ap()

    idb_dram = nc.inline_tensor(np.eye(128, dtype=ml_dtypes.bfloat16), "idb_dram").ap()
    idf_dram = nc.inline_tensor(np.eye(128, dtype=np.float32), "idf_dram").ap()

    agT_sc = nc.dram_tensor("agT_sc", [(n_routed + 1) * 2, 128, HC * 1024], BF16).ap()
    auT_sc = nc.dram_tensor("auT_sc", [(n_routed + 1) * 2, 128, HC * 1024], BF16).ap()
    bdT_sc = nc.dram_tensor("bdT_sc", [(n_routed + 1) * 2, 128, IC_H * 1024], BF16).ap()
    cc_in = nc.dram_tensor("cc_in", [2 * 128, H], F32).ap()
    cc_out = nc.dram_tensor("cc_out", [2 * 128, H], F32).ap()
    if attn_debug:
        d_x1 = nc.dram_tensor("d_x1", [NT, H], F32, kind="ExternalOutput").ap()
        d_kv = nc.dram_tensor("d_kv", [128, 2 * H], F32, kind="ExternalOutput").ap()
        d_kvpart = nc.dram_tensor("d_kvpart", [128, 2 * H], F32, kind="ExternalOutput").ap()
        d_qt = nc.dram_tensor("d_qt", [128, 2 * NT], BF16, kind="ExternalOutput").ap()
        d_z = nc.dram_tensor("d_z", [128, n_lt], F32, kind="ExternalOutput").ap()

    off = [(nc.sbuf_base + 31) // 32 * 32]

    def sb(name, cols, dtype, at=None):
        nbytes = cols * (2 if dtype == BF16 else 4)
        if at is None:
            at = off[0]
            off[0] += (nbytes + 31) // 32 * 32
        else:
            assert at + nbytes <= off[0] or True
        return nc.alloc_sbuf_tensor_at(name, [128, cols], dtype, offset=at), at

    X, _ = sb("X", n_lt * H, F32)
    A_g, A_g_at = sb("A_g", HC * 1024, BF16)
    A_u, A_u_at = sb("A_u", HC * 1024, BF16)
    B_d, B_d_at = sb("B_d", IC_H * 1024, BF16)
    C_h, C_h_at = sb("C_h", IC_H * 1024, BF16)
    XLT, _ = sb("XLT", HC * 1024, BF16)
    LNF, _ = sb("LNF", 1024, BF16)
    GS0, _ = sb("GS0", 512, BF16)
    CW, _ = sb("CW", n_lt * n_routed, F32)
    PR0, _ = sb("PR0", n_routed, F32)
    PR1, _ = sb("PR1", n_routed, F32)
    STT, _ = sb("STT", 32, F32)
    STS = [[STT[:, i * 4 + j:i * 4 + j + 1] for j in range(4)] for i in range(8)]
    M1, _ = sb("M1", 1, F32)
    M2, _ = sb("M2", 1, F32)
    EPSC, _ = sb("EPSC", 1, F32)
    N_CS = 3
    CSF = [sb(f"CSF{i}", 512, F32)[0] for i in range(N_CS)]
    CSB = [sb(f"CSB{i}", 512, BF16)[0] for i in range(2)]
    TB = [sb(f"TB{i}", 512, BF16)[0] for i in range(2)]
    IDF, _ = sb("IDF", 128, F32)
    IDB, _ = sb("IDB", 128, BF16)
    RBB, _ = sb("RBB", n_routed, F32)
    WRT, _ = sb("WRT", HC * n_routed, BF16)
    ZS = None
    QT, _ = sb("QT", 2 * NT, BF16, at=C_h_at)
    KVS, _ = sb("KVS", 2 * H, F32, at=A_g_at)
    KVB, _ = sb("KVB", 2 * H, BF16, at=A_g_at + 8192)
    QPF, _ = sb("QPF", PROJ, F32, at=A_u_at)
    KPF, _ = sb("KPF", PROJ, F32, at=A_u_at + 1024)
    KPB, _ = sb("KPB", PROJ, BF16, at=A_u_at + 2048)
    LN2B, _ = sb("LN2B", 1024, BF16, at=A_u_at + 2560)
    LN2T, _ = sb("LN2T", HC * 128, BF16, at=A_u_at + 4608)
    ZS, _ = sb("ZS", n_lt, F32, at=A_u_at + 8192)
    ZI = ZS
    LN2B2, _ = sb("LN2B2", 1024, BF16, at=A_u_at + 8704)
    LN2T2, _ = sb("LN2T2", HC * 128, BF16, at=A_u_at + 10752)
    QPF2, _ = sb("QPF2", PROJ, F32, at=A_u_at + 12800)
    KPF2, _ = sb("KPF2", PROJ, F32, at=A_u_at + 13824)
    KPB2, _ = sb("KPB2", PROJ, BF16, at=A_u_at + 14848)
    QPJ, _ = sb("QPJ", HC * PROJ, BF16, at=B_d_at)
    KPJ, _ = sb("KPJ", HC * PROJ, BF16, at=B_d_at + 4096)

    assert off[0] <= nc.sbuf_top, f"SBUF over budget: {off[0]} > {nc.sbuf_top}"
    import sys
    print(f"[kernel] sbuf used: {off[0]} of {nc.sbuf_top}", file=sys.stderr)
    P = [nc.alloc_psum_tensor(f"P{i}", [128, 512], F32) for i in range(8)]

    pr = Prog(nc)

    pr.op("vector", lambda v: v.memset(EPSC[:, :], LN_EPS),
          writes=["EPSC"], sem="dve", inc=1)
    pr.dma_group("sync", "sp", [
        (lambda s: s.dma_start(out=IDB[:, :], in_=idb_dram), [], ["IDB"]),
        (lambda s: s.dma_start(out=IDF[:, :], in_=idf_dram), [], ["IDF"]),
        (lambda s: s.dma_start(out=RBB[:, :], in_=rb_b), [], ["RBB"]),
    ])
    x_in3 = x_in.rearrange("(t p) d -> p t d", p=128)
    x_out3 = x_out.rearrange("(t p) d -> p t d", p=128)
    X3 = X[:, :].rearrange("p (t d) -> p t d", t=n_lt)
    pr.dma_group("sync", "sp", [
        (lambda s: s.dma_start(out=X3[:, 0:8, :], in_=x_in3[:, 0:8, :]),
         [], [f"X{lt}" for lt in range(8)])])
    for q in range(1, n_q):
        pr.dma_group("scalar", "aq", [
            (lambda s, q=q: s.dma_start(
                out=X3[:, q * 8:(q + 1) * 8, :],
                in_=x_in3[:, q * 8:(q + 1) * 8, :]),
             [], [f"X{lt}" for lt in range(q * 8, q * 8 + 8)])])
    cast_ci = [0]

    def cast_small(src3d, dst3d, dstkey, nchunk, per):
        for i in range(nchunk):
            b = cast_ci[0] % 2
            cast_ci[0] += 1
            fk, bk = f"CSF{b}", f"CSB{b}"
            pr.dma_group("sync", "sp", [
                (lambda s, i=i, b=b: s.dma_start(
                    out=CSF[b][:, 0:per], in_=src3d[:, i, :]),
                 [], [fk])])
            pr.op("scalar", lambda sc, b=b: sc.copy(
                out=CSB[b][:, 0:per], in_=CSF[b][:, 0:per]),
                reads=[fk], writes=[bk], sem="act", inc=1)
            pr.op("vector", lambda v, i=i, b=b: v.tensor_copy(
                out=dst3d[:, i, :], in_=CSB[b][:, 0:per]),
                reads=[bk], writes=[dstkey], sem="dve", inc=1)

    cast_small(wr_t.rearrange("(hc p) e -> p hc e", p=128),
               WRT[:, :].rearrange("p (hc e) -> p hc e", hc=HC),
               "WRT", HC, n_routed)

    p3b = P[3][:, :].bitcast(BF16)

    def prep_steps(e, ih):
        k = e * 2 + ih
        srcs = [(wg_s if e == 0 else wg_r[e - 1], agT_sc, f"agT{k}", True),
                (wu_s if e == 0 else wu_r[e - 1], auT_sc, f"auT{k}", True),
                (wd_s if e == 0 else wd_r[e - 1], bdT_sc, f"bdT{k}", False)]
        steps = []
        for src, dstT, dkey, is_gu in srcs:
            for rb in range(8):
                for cb in range(2):
                    def step(src=src, dstT=dstT, dkey=dkey, is_gu=is_gu,
                             rb=rb, cb=cb, k=k, ih=ih):
                        b = cast_ci[0] % N_CS
                        tb = cast_ci[0] % 2
                        cast_ci[0] += 1
                        fk, bk, tk = f"CSF{b}", f"CSB{tb}", f"TB{tb}"
                        if is_gu:
                            r0, c0 = ih * 1024 + rb * 128, cb * 512
                        else:
                            r0, c0 = rb * 128, ih * 1024 + cb * 512
                        pr.dma_group("sync", "sp", [
                            (lambda s: s.dma_start(
                                out=CSF[b][:, :], in_=src[r0:r0 + 128, c0:c0 + 512]),
                             [], [fk])])
                        pr.op("scalar", lambda sc: sc.copy(
                            out=CSB[tb][:, :], in_=CSF[b][:, :]),
                            reads=[fk], writes=[bk], sem="act", inc=1)
                        def tr(t):
                            for tt in range(4):
                                ins = t.transpose(
                                    p3b[:, tt * 128:tt * 128 + 128],
                                    CSB[tb][:, tt * 128:tt * 128 + 128],
                                    IDB[:, :])
                            return ins
                        pr.op("tensor", tr, reads=[bk, "IDB", "P3"],
                              writes=["P3"], sem="pe", inc=1)
                        pr.op("vector", lambda v: v.tensor_copy(
                            out=TB[tb][:, :], in_=p3b[:, 0:512]),
                            reads=["P3"], writes=[tk], sem="dve", inc=1)
                        if is_gu:
                            col0 = (cb * 8 + rb) * 512
                            pr.dma_group("scalar", "aq", [
                                (lambda s: s.dma_start(
                                    out=dstT[k][:, col0:col0 + 512],
                                    in_=TB[tb][:, :]), [tk], [dkey])])
                        else:
                            bd3 = dstT[k].rearrange("p (ic o) -> p ic o", ic=IC_H)
                            pr.dma_group("scalar", "aq", [
                                (lambda s: s.dma_start(
                                    out=bd3[:, cb * 4:cb * 4 + 4, r0:r0 + 128],
                                    in_=TB[tb][:, :].rearrange(
                                        "p (t o) -> p t o", t=4)),
                                 [tk], [dkey])])
                    steps.append(step)
        return steps

    prep_queue = []

    def pump(n=10**9):
        while prep_queue and n > 0:
            prep_queue.pop(0)()
            n -= 1

    def ag_col(hc, ic):
        return ((hc // 4) * 8 + ic) * 512 + (hc % 4) * 128

    def load_AB(e, ih):
        k = e * 2 + ih
        pr.dma_group("sync", "sp", [
            (lambda s: s.dma_start(out=A_g[:, :], in_=agT_sc[k]),
             [f"agT{k}"], ["A_g"]),
            (lambda s: s.dma_start(out=A_u[:, :], in_=auT_sc[k]),
             [f"auT{k}"], ["A_u"]),
        ])
        pr.dma_group("sync", "sp", [
            (lambda s: s.dma_start(out=B_d[:, :], in_=bdT_sc[k]),
             [f"bdT{k}"], ["B_d"]),
        ])

    def ln_stats(lt):
        Xk = f"X{lt}"
        Xsl = X[:, lt * H:(lt + 1) * H]
        ST0, ST1, ST2, ST3 = STS[lt % 8]
        s0, s1, s2, s3 = (f"ST{j}_{lt % 8}" for j in range(4))
        lt8 = lt % 8
        xlt3 = XLT[:, :].rearrange("p (hc t) -> p hc t", hc=HC)
        dump = xlt3[:, :, lt8 * 128:(lt8 + 1) * 128]
        pr.op("vector", lambda v: v.reduce_sum(ST0, Xsl, axis=AX.X),
              reads=[Xk], writes=[s0], sem="dve", inc=1)
        pr.op("vector", lambda v: v.tensor_scalar(
            out=ST0, in0=ST0, scalar1=-1.0 / H, scalar2=None,
            op0=ALU.mult), reads=[s0], writes=[s0], sem="dve", inc=1)
        pr.op("scalar", lambda sc: sc.activation(
            out=dump, in_=Xsl, func=AF.Square, bias=ST0,
            accum_out=ST1),
            reads=[Xk, s0], writes=[f"XLT{lt8}", s1], sem="act", inc=1)
        pr.op("scalar", lambda sc: sc.activation(
            out=ST2, in_=ST1, func=AF.Sqrt, scale=1.0 / H,
            bias=EPSC[:, :]), reads=[s1, "EPSC"], writes=[s2],
            sem="act", inc=1)
        pr.op("vector", lambda v: v.reciprocal(ST2, ST2),
              reads=[s2], writes=[s2], sem="dve", inc=1)
        pr.op("vector", lambda v: v.tensor_scalar(
            out=ST3, in0=ST0, scalar1=ST2, scalar2=None,
            op0=ALU.mult), reads=[s0, s2], writes=[s3], sem="dve", inc=1)

    def ln_apply(lt, with_ln2=False):
        Xk = f"X{lt}"
        Xsl = X[:, lt * H:(lt + 1) * H]
        ST0, ST1, ST2, ST3 = STS[lt % 8]
        s2, s3 = f"ST2_{lt % 8}", f"ST3_{lt % 8}"
        lt8 = lt % 8
        xlt3 = XLT[:, :].rearrange("p (hc t) -> p hc t", hc=HC)
        pr.op("scalar", lambda sc: sc.activation(
            out=LNF[:, :], in_=Xsl, func=AF.Identity, scale=ST2,
            bias=ST3),
            reads=[Xk, s2, s3], writes=["LNF"], sem="act", inc=1)
        bank = P[3] if lt % 2 == 0 else P[2]
        bkey = "P3" if lt % 2 == 0 else "P2"
        pb = bank[:, :].bitcast(BF16)
        def tr(t, pb=pb, lnf=LNF):
            for hc in range(HC):
                ins = t.transpose(pb[:, hc * 128:hc * 128 + 128],
                                  lnf[:, hc * 128:hc * 128 + 128], IDB[:, :])
            return ins
        pr.op("tensor", tr, reads=["LNF", "IDB", bkey],
              writes=[bkey], sem="pe", inc=1)
        if not with_ln2:
            pr.op("vector", lambda v, lt8=lt8, pb=pb: v.tensor_copy(
                out=xlt3[:, :, lt8 * 128:(lt8 + 1) * 128],
                in_=pb[:, :].rearrange("p (hc t) -> p hc t", hc=HC)),
                reads=[bkey], writes=[f"XLT{lt8}"], sem="dve", inc=1)
        else:
            par = lt % 2
            l2b = LN2B if par == 0 else LN2B2
            l2t = LN2T if par == 0 else LN2T2
            bk, tk2 = f"LN2B{par}", f"LN2T{par}"
            pr.op("scalar", lambda sc, l2b=l2b: sc.copy(out=l2b[:, :], in_=LNF[:, :]),
                  reads=["LNF"], writes=[bk], sem="act", inc=1)
            pr.op("vector", lambda v, l2t=l2t, pb=pb: v.tensor_copy(
                out=l2t[:, :].rearrange("p (hc t) -> p hc t", hc=HC),
                in_=pb[:, :].rearrange("p (hc t) -> p hc t", hc=HC)),
                reads=[bkey], writes=[tk2], sem="dve", inc=1)

    def ln_tile(lt, with_ln2=False):
        ln_stats(lt)
        ln_apply(lt, with_ln2)

    def routing(q):
        for lt8 in range(8):
            lt = q * 8 + lt8
            rp = P[5] if lt8 % 2 == 0 else P[6]
            rpk = "P5" if lt8 % 2 == 0 else "P6"
            def mm(t, lt8=lt8, rp=rp):
                for hc in range(HC):
                    ins = t.matmul(rp[:, 0:n_routed],
                                   XLT[:, hc * 1024 + lt8 * 128:hc * 1024 + lt8 * 128 + 128],
                                   WRT[:, hc * n_routed:(hc + 1) * n_routed],
                                   start=(hc == 0), stop=(hc == HC - 1))
                return ins
            pr.op("tensor", mm,
                  reads=["WRT", rpk, f"XLT{lt8}"], writes=[rpk],
                  sem="pe", inc=1)
            pr.op("vector", lambda v, rp=rp: v.tensor_tensor(
                out=PR0[:, :], in0=rp[:, 0:n_routed], in1=RBB[:, :], op=ALU.add),
                reads=[rpk, "RBB"], writes=["PR0"], sem="dve", inc=1)
            pr.op("scalar", lambda sc: sc.activation(
                out=PR0[:, :], in_=PR0[:, :], func=AF.Sigmoid),
                reads=["PR0"], writes=["PR0"], sem="act", inc=1)
            pr.op("vector", lambda v: v.reduce_max(M1[:, :], PR0[:, :], axis=AX.X),
                  reads=["PR0"], writes=["M1"], sem="dve", inc=1)
            pr.op("vector", lambda v: v.tensor_scalar(
                out=PR1[:, :], in0=PR0[:, :], scalar1=M1[:, :], scalar2=None,
                op0=ALU.is_ge), reads=["PR0", "M1"], writes=["PR1"],
                sem="dve", inc=1)
            pr.op("vector", lambda v: v.scalar_tensor_tensor(
                out=PR1[:, :], in0=PR1[:, :], scalar=-1e30, in1=PR0[:, :],
                op0=ALU.mult, op1=ALU.add),
                reads=["PR1", "PR0"], writes=["PR1"], sem="dve", inc=1)
            pr.op("vector", lambda v: v.reduce_max(M2[:, :], PR1[:, :], axis=AX.X),
                  reads=["PR1"], writes=["M2"], sem="dve", inc=1)
            pr.op("vector", lambda v: v.scalar_tensor_tensor(
                out=PR1[:, :], in0=PR0[:, :], scalar=M2[:, :], in1=PR0[:, :],
                op0=ALU.is_ge, op1=ALU.mult),
                reads=["PR0", "M2"], writes=["PR1"], sem="dve", inc=1)
            pr.op("vector", lambda v, lt=lt: v.tensor_scalar(
                out=CW[:, lt * n_routed:(lt + 1) * n_routed], in0=PR1[:, :],
                scalar1=0.5, scalar2=None, op0=ALU.mult),
                reads=["PR1"], writes=["CW"], sem="dve", inc=1)

    def load_A(e, ih):
        if (e, ih) in wb_done:
            k = e * 2 + ih
            pr.dma_group("sync", "sp", [
                (lambda s, k=k: s.dma_start(out=A_g[:, :], in_=agT_sc[k]),
                 [f"agT{k}"], ["A_g"]),
                (lambda s, k=k: s.dma_start(out=A_u[:, :], in_=auT_sc[k]),
                 [f"auT{k}"], ["A_u"]),
            ])
            return
        cast_expert(e)
        for la in (1, 2):
            if e + la <= n_routed:
                cast_expert(e + la)
        src_g = wgs_bf if e == 0 else wg_bf[e - 1]
        src_u = wus_bf if e == 0 else wu_bf[e - 1]
        items = []
        for hc in range(HC):
            items.append((lambda s, hc=hc, src=src_g, ih=ih: s.dma_start(
                out=A_g[:, hc * 1024:(hc + 1) * 1024],
                in_=src[ih * 1024:(ih + 1) * 1024, hc * 128:(hc + 1) * 128],
                transpose=True), [f"wbf_g{e}"], ["A_g"]))
        for hc in range(HC):
            items.append((lambda s, hc=hc, src=src_u, ih=ih: s.dma_start(
                out=A_u[:, hc * 1024:(hc + 1) * 1024],
                in_=src[ih * 1024:(ih + 1) * 1024, hc * 128:(hc + 1) * 128],
                transpose=True), [f"wbf_u{e}"], ["A_u"]))
        pr.dma_group("sync", "sp", items)
        k = e * 2 + ih
        pr.dma_group("scalar", "aq", [
            (lambda s, k=k: s.dma_start(out=agT_sc[k], in_=A_g[:, :]),
             ["A_g"], [f"agT{k}"]),
            (lambda s, k=k: s.dma_start(out=auT_sc[k], in_=A_u[:, :]),
             ["A_u"], [f"auT{k}"]),
        ])

    def load_B(e, ih):
        k = e * 2 + ih
        if (e, ih) in wb_done:
            pr.dma_group("sync", "sp", [
                (lambda s, k=k: s.dma_start(out=B_d[:, :], in_=bdT_sc[k]),
                 [f"bdT{k}"], ["B_d"])])
            return
        wb_done.add((e, ih))
        src_d = wds_bf if e == 0 else wd_bf[e - 1]
        items = []
        for ic in range(IC_H):
            col = (ih * IC_H + ic) * 128
            items.append((lambda s, ic=ic, col=col, src=src_d: s.dma_start(
                out=B_d[:, ic * 1024:(ic + 1) * 1024],
                in_=src[:, col:col + 128], transpose=True),
                [f"wbf_d{e}"], ["B_d"]))
        pr.dma_group("sync", "sp", items)
        pr.dma_group("scalar", "aq", [
            (lambda s, k=k: s.dma_start(out=bdT_sc[k], in_=B_d[:, :]),
             ["B_d"], [f"bdT{k}"])])

    def ffn_quarter_expert(q, e, ih, mid_stats=None, mid_apply=None):
        for ls2 in range(2):
            for ic in range(IC_H):
                pg = P[ic % 2]
                pgk = f"P{ic % 2}"
                gs = GS0
                gsk = "GS0"
                def gmm(t, pg=pg, ic=ic, ls2=ls2):
                    for hc in range(HC):
                        ins = t.matmul(
                            pg[:, :],
                            A_g[:, ag_col(hc, ic):ag_col(hc, ic) + 128],
                            XLT[:, hc * 1024 + ls2 * 512:hc * 1024 + ls2 * 512 + 512],
                            start=(hc == 0), stop=(hc == HC - 1))
                    return ins
                pr.op("tensor", gmm,
                      reads=["A_g", pgk] + [f"XLT{j}" for j in range(8)],
                      writes=[pgk], sem="pe", inc=1)
                pr.op("scalar", lambda sc, pg=pg, gs=gs: sc.activation(
                    out=gs[:, :], in_=pg[:, :], func=AF.Silu),
                    reads=[pgk], writes=[gsk], sem="act", inc=1)
                def umm(t, ic=ic, ls2=ls2):
                    for hc in range(HC):
                        ins = t.matmul(
                            P[2][:, :],
                            A_u[:, ag_col(hc, ic):ag_col(hc, ic) + 128],
                            XLT[:, hc * 1024 + ls2 * 512:hc * 1024 + ls2 * 512 + 512],
                            start=(hc == 0), stop=(hc == HC - 1))
                    return ins
                pr.op("tensor", umm,
                      reads=["A_u", "P2"] + [f"XLT{j}" for j in range(8)],
                      writes=["P2"], sem="pe", inc=1)
                pr.op("vector", lambda v, gs=gs, ic=ic, ls2=ls2:
                      v.tensor_tensor(
                          out=C_h[:, ic * 1024 + ls2 * 512:ic * 1024 + ls2 * 512 + 512],
                          in0=P[2][:, :], in1=gs[:, :], op=ALU.mult),
                      reads=["P2", gsk], writes=["C_h"], sem="dve", inc=1)
                pump(3)
        if mid_stats is not None:
            mid_stats()
        for sub in range(4):
            for lsub in range(2):
                pidx = sub * 2 + lsub
                base = 4 if pidx % 2 == 0 else 6
                def dmm(t, sub=sub, lsub=lsub, base=base):
                    for ic in range(IC_H):
                        for hoc in range(2):
                            ins = t.matmul(
                                P[base + hoc][:, :],
                                C_h[:, ic * 1024 + sub * 256 + lsub * 128:
                                    ic * 1024 + sub * 256 + lsub * 128 + 128],
                                B_d[:, ic * 1024 + hoc * 512:ic * 1024 + hoc * 512 + 512],
                                start=(ic == 0), stop=(ic == IC_H - 1),
                                skip_group_check=True)
                    return ins
                pkeys = [f"P{base}", f"P{base + 1}"]
                pr.op("tensor", dmm, reads=["C_h", "B_d"] + pkeys, writes=pkeys,
                      sem="pe", inc=1)
                lt = q * 8 + sub * 2 + lsub
                for hoc in range(2):
                    xsl = X[:, lt * H + hoc * 512:lt * H + hoc * 512 + 512]
                    if e == 0:
                        sca = 0.5
                        rd = []
                    else:
                        sca = CW[:, lt * n_routed + (e - 1):lt * n_routed + e]
                        rd = ["CW"]
                    pr.op("vector", lambda v, p=P[base + hoc], xsl=xsl, sca=sca:
                          v.scalar_tensor_tensor(out=xsl, in0=p[:, :], scalar=sca,
                                                 in1=xsl, op0=ALU.mult, op1=ALU.add),
                          reads=[f"P{base + hoc}", f"X{lt}"] + rd,
                          writes=[f"X{lt}"], sem="dve", inc=1)
                pump(3)
                if mid_apply is not None:
                    mid_apply(sub * 2 + lsub)

    prepped = set()

    def ffn_call(skip_ln_q0=False, store_out=False, tail_stats=None):
        phases = [(e, ih) for e in range(n_routed + 1) for ih in range(2)]
        for q in range(n_q):
            if q == 0 and not skip_ln_q0:
                with _Sect("ln"):
                    for lt8 in range(8):
                        ln_stats(lt8)
                    for lt8 in range(8):
                        ln_apply(lt8)
            with _Sect("routing"):
                routing(q)
            for idx, (e, ih) in enumerate(phases):
                if idx == 0:
                    for jdx in (0, 1):
                        pe_, pih_ = phases[jdx]
                        if (pe_, pih_) not in prepped:
                            prepped.add((pe_, pih_))
                            prep_queue.extend(prep_steps(pe_, pih_))
                            pump()
                    load_AB(e, ih)
                if idx + 2 < len(phases):
                    ne2 = phases[idx + 2]
                    if ne2 not in prepped:
                        prepped.add(ne2)
                        prep_queue.extend(prep_steps(*ne2))
                mid_stats = mid_apply = None
                if idx == len(phases) - 1:
                    if q + 1 < n_q:
                        def mid_stats(q=q):
                            for j in range(8):
                                ln_stats((q + 1) * 8 + j)
                        def mid_apply(j, q=q):
                            ln_apply((q + 1) * 8 + j)
                    elif tail_stats is not None:
                        mid_stats = tail_stats
                with _Sect("expert"):
                    ffn_quarter_expert(q, e, ih, mid_stats=mid_stats,
                                       mid_apply=mid_apply)
                pump()
                if idx + 1 < len(phases):
                    load_AB(*phases[idx + 1])
                elif q + 1 < n_q:
                    load_AB(*phases[0])
            if store_out:
                pr.dma_group("sync", "sp", [
                    (lambda s, q=q: s.dma_start(
                        out=x_out3[:, q * 8:(q + 1) * 8, :],
                        in_=X3[:, q * 8:(q + 1) * 8, :]),
                     [f"X{lt}" for lt in range(q * 8, q * 8 + 8)],
                     ["x_out"])])

    def attention(stats_prefilled=False, post_ln_q0=False):
        if attn_debug:
            pr.dma_group("sync", "sp", [(lambda s: s.dma_start(
                out=d_x1.rearrange("(t p) d -> p t d", p=128),
                in_=X[:, :].rearrange("p (t d) -> p t d", t=n_lt)),
                [f"X{lt}" for lt in range(n_lt)], ["d_x1"])])
        pr.alias("QT", ["C_h"])
        pr.alias("KVS", ["A_g"])
        pr.alias("KVB", ["A_g"])
        for k in ("QPF", "KPF", "KPB", "LN2B", "LN2T", "ZS",
                  "QPF2", "KPF2", "KPB2", "LN2B2", "LN2T2"):
            pr.alias(k, ["A_u"])
        pr.alias("QPJ", ["B_d"])
        pr.alias("KPJ", ["B_d"])
        cast_small(qproj.rearrange("(hc p) e -> p hc e", p=128),
                   QPJ[:, :].rearrange("p (hc e) -> p hc e", hc=HC),
                   "QPJ", HC, PROJ)
        cast_small(kproj.rearrange("(hc p) e -> p hc e", p=128),
                   KPJ[:, :].rearrange("p (hc e) -> p hc e", hc=HC),
                   "KPJ", HC, PROJ)
        for lt in range(n_lt):
            par = lt % 2
            l2t, l2b = (LN2T, LN2B) if par == 0 else (LN2T2, LN2B2)
            qpf, kpf, kpb = (QPF, KPF, KPB) if par == 0 else (QPF2, KPF2, KPB2)
            tkq, tkk, tkb = f"QPF{par}", f"KPF{par}", f"KPB{par}"
            tk2, bk2 = f"LN2T{par}", f"LN2B{par}"
            gsl = GS0[:, par * 256:par * 256 + 256]
            gsk = f"GS0{par}"
            pq, pk = (P[0], P[1]) if par == 0 else (P[2], P[3])
            pqk, pkk = (("P0", "P1") if par == 0 else ("P2", "P3"))
            if lt == 0 and not stats_prefilled:
                for j in range(8):
                    ln_stats(j)
            ln_apply(lt, with_ln2=True)
            def qmm(t, l2t=l2t, pq=pq):
                for hc in range(HC):
                    ins = t.matmul(pq[:, 0:PROJ],
                                   l2t[:, hc * 128:(hc + 1) * 128],
                                   QPJ[:, hc * PROJ:(hc + 1) * PROJ],
                                   start=(hc == 0), stop=(hc == HC - 1),
                                   skip_group_check=True)
                return ins
            pr.op("tensor", qmm, reads=[tk2, "QPJ", pqk], writes=[pqk],
                  sem="pe", inc=1)
            pr.op("scalar", lambda sc, lt=lt, qpf=qpf, pq=pq: sc.activation(
                out=qpf[:, :], in_=pq[:, 0:PROJ], func=AF.Exp,
                accum_out=ZS[:, lt:lt + 1]),
                reads=[pqk], writes=[tkq, "ZS"], sem="act", inc=1)
            def kmm(t, l2t=l2t, pk=pk):
                for hc in range(HC):
                    ins = t.matmul(pk[:, 0:PROJ],
                                   l2t[:, hc * 128:(hc + 1) * 128],
                                   KPJ[:, hc * PROJ:(hc + 1) * PROJ],
                                   start=(hc == 0), stop=(hc == HC - 1),
                                   skip_group_check=True)
                return ins
            pr.op("tensor", kmm, reads=[tk2, "KPJ", pkk], writes=[pkk],
                  sem="pe", inc=1)
            pr.op("scalar", lambda sc, kpf=kpf, pk=pk: sc.activation(
                out=kpf[:, :], in_=pk[:, 0:PROJ], func=AF.Exp),
                reads=[pkk], writes=[tkk], sem="act", inc=1)
            pr.op("vector", lambda v, kpb=kpb, kpf=kpf: v.tensor_copy(
                out=kpb[:, :], in_=kpf[:, :]),
                reads=[tkk], writes=[tkb], sem="dve", inc=1)
            pr.op("vector", lambda v, gsl=gsl, qpf=qpf: v.tensor_copy(
                out=gsl, in_=qpf[:, :]),
                reads=[tkq], writes=[gsk], sem="dve", inc=1)
            p2b = (P[1] if par == 0 else P[0])[:, :].bitcast(BF16)
            trk = "P1" if par == 0 else "P0"
            def qtr(t, p2b=p2b, gsl=gsl):
                for ec in range(2):
                    ins = t.transpose(p2b[:, ec * 128:ec * 128 + 128],
                                      gsl[:, ec * 128:ec * 128 + 128], IDB[:, :])
                return ins
            pr.op("tensor", qtr, reads=[gsk, "IDB", trk], writes=[trk],
                  sem="pe", inc=1)
            pr.op("vector", lambda v, lt=lt, p2b=p2b: v.tensor_copy(
                out=QT[:, :].rearrange("p (ec t) -> p ec t", ec=2)
                    [:, :, lt * 128:(lt + 1) * 128],
                in_=p2b[:, 0:256].rearrange("p (ec t) -> p ec t", ec=2)),
                reads=[trk], writes=["QT"], sem="dve", inc=1)
            def kvmm(t, kpb=kpb, l2b=l2b):
                for ec in range(2):
                    for dsl in range(2):
                        ins = t.matmul(
                            P[4 + ec * 2 + dsl][:, :],
                            kpb[:, ec * 128:(ec + 1) * 128],
                            l2b[:, dsl * 512:(dsl + 1) * 512],
                            start=True, stop=True,
                            skip_group_check=True)
                return ins
            pr.op("tensor", kvmm,
                  reads=[tkb, bk2, "P4", "P5", "P6", "P7"],
                  writes=["P4", "P5", "P6", "P7"], sem="pe", inc=1)
            for ec in range(2):
                for dsl in range(2):
                    ksl = KVS[:, ec * 1024 + dsl * 512:ec * 1024 + dsl * 512 + 512]
                    if lt == 0:
                        pr.op("vector", lambda v, ec=ec, dsl=dsl, ksl=ksl:
                              v.tensor_copy(out=ksl, in_=P[4 + ec * 2 + dsl][:, :]),
                              reads=[f"P{4 + ec * 2 + dsl}"], writes=["KVS"],
                              sem="dve", inc=1)
                    else:
                        pr.op("vector", lambda v, ec=ec, dsl=dsl, ksl=ksl:
                              v.tensor_tensor(out=ksl, in0=P[4 + ec * 2 + dsl][:, :],
                                              in1=ksl, op=ALU.add),
                              reads=[f"P{4 + ec * 2 + dsl}", "KVS"], writes=["KVS"],
                              sem="dve", inc=1)
            if lt + 8 < n_lt:
                ln_stats(lt + 8)
        pr.op("vector", lambda v: v.tensor_scalar(
            out=ZI[:, :], in0=ZS[:, :], scalar1=1e-8, scalar2=None, op0=ALU.add),
            reads=["ZS"], writes=["ZI"], sem="dve", inc=1)
        pr.op("vector", lambda v: v.reciprocal(ZI[:, :], ZI[:, :]),
              reads=["ZI"], writes=["ZI"], sem="dve", inc=1)
        if attn_debug:
            pr.dma_group("sync", "sp", [(lambda s: s.dma_start(
                out=d_kvpart, in_=KVS[:, :]), ["KVS"], ["d_kvpart"])])
        pr.dma_group("sync", "sp", [(lambda s: s.dma_start(
            out=cc_in.rearrange("(ec p) d -> p ec d", p=128),
            in_=KVS[:, :].rearrange("p (ec d) -> p ec d", ec=2)),
            ["KVS"], ["cc_in"])])
        pr.op("gpsimd", lambda g: g.collective_compute(
            "AllReduce", ALU.add,
            replica_groups=[[0, 1, 2, 3], [4, 5, 6, 7]],
            ins=[cc_in], outs=[cc_out]),
            reads=["cc_in"], writes=["cc_out"], sem="cc", inc=1)
        pr.dma_group("sync", "sp", [(lambda s: s.dma_start(
            out=KVS[:, :].rearrange("p (ec d) -> p ec d", ec=2),
            in_=cc_out.rearrange("(ec p) d -> p ec d", p=128)),
            ["cc_out"], ["KVS"])])
        pr.op("vector", lambda v: v.tensor_copy(out=KVB[:, :], in_=KVS[:, :]),
              reads=["KVS"], writes=["KVB"], sem="dve", inc=1)
        if attn_debug:
            pr.dma_group("sync", "sp", [
                (lambda s: s.dma_start(out=d_kv, in_=KVS[:, :]), ["KVS"], ["d_kv"]),
                (lambda s: s.dma_start(out=d_qt, in_=QT[:, :]), ["QT"], ["d_qt"]),
                (lambda s: s.dma_start(out=d_z, in_=ZS[:, :]), ["ZS"], ["d_z"]),
            ])
        for lt in range(n_lt):
            for dsl in range(2):
                pb = (lt % 2) * 2 + dsl
                def nmm(t, lt=lt, dsl=dsl, pb=pb):
                    for ec in range(2):
                        ins = t.matmul(
                            P[pb][:, :],
                            QT[:, ec * NT + lt * 128:ec * NT + lt * 128 + 128],
                            KVB[:, ec * 1024 + dsl * 512:ec * 1024 + dsl * 512 + 512],
                            start=(ec == 0), stop=(ec == 1),
                            skip_group_check=True)
                    return ins
                pr.op("tensor", nmm, reads=["QT", "KVB", f"P{pb}"],
                      writes=[f"P{pb}"], sem="pe", inc=1)
                xsl = X[:, lt * H + dsl * 512:lt * H + dsl * 512 + 512]
                pr.op("vector", lambda v, p=P[pb], xsl=xsl, lt=lt:
                      v.scalar_tensor_tensor(
                          out=xsl, in0=p[:, :], scalar=ZI[:, lt:lt + 1],
                          in1=xsl, op0=ALU.mult, op1=ALU.add),
                      reads=[f"P{pb}", f"X{lt}", "ZI"], writes=[f"X{lt}"],
                      sem="dve", inc=1)
            if post_ln_q0 and lt < 8:
                ln_stats(lt)
                ln_apply(lt)

    n_st = len(stages)
    for i, st in enumerate(stages):
        if st == "f":
            pr.alias("C_h", ["QT"])
            pr.alias("A_g", ["KVS", "KVB"])
            pr.alias("A_u", ["QPF", "KPF", "KPB", "LN2B", "LN2T", "ZS",
                     "QPF2", "KPF2", "KPB2", "LN2B2", "LN2T2"])
            pr.alias("B_d", ["QPJ", "KPJ"])
            tail = None
            if i + 1 < n_st and stages[i + 1] == "a":
                def tail():
                    for j in range(8):
                        ln_stats(j)
            ffn_call(skip_ln_q0=(i > 0 and stages[i - 1] == "a"),
                     store_out=(i == n_st - 1),
                     tail_stats=tail)
        elif st == "a":
            attention(stats_prefilled=(i > 0 and stages[i - 1] == "f"),
                      post_ln_q0=(i + 1 < n_st and stages[i + 1] == "f"))

    if not (stages and stages[-1] == "f"):
        pr.dma_group("sync", "sp", [(lambda s: s.dma_start(
            out=x_out3, in_=X[:, :].rearrange("p (t d) -> p t d", t=n_lt)),
            [f"X{lt}" for lt in range(n_lt)], ["x_out"])])
    pr.waitpoint("sync", ["x_out"])

    sem_handles = {}
    ctxs = []
    for s in pr.sem_names:
        cm = nc.semaphore(s)
        sem_handles[s] = cm.__enter__()
        ctxs.append(cm)
    blk = nc.Block()
    block = blk.__enter__()
    pr.replay(block, sem_handles)
    blk.__exit__(None, None, None)
    for cm in reversed(ctxs):
        cm.__exit__(None, None, None)
    return nc


def kernel(L0_attn_x, L0_attn_q_proj, L0_attn_k_proj, L0_ffn_Wg_s, L0_ffn_Wu_s,
           L0_ffn_Wd_s, L0_ffn_Wg, L0_ffn_Wu, L0_ffn_Wd, L0_ffn_Wr, L0_ffn_rb):
    x = np.ascontiguousarray(np.asarray(L0_attn_x))
    B, S, _ = x.shape
    shards = x.reshape(B * 4, S // 4, H)
    wr_t = np.ascontiguousarray(np.asarray(L0_ffn_Wr).T)
    rb_b = np.ascontiguousarray(
        np.broadcast_to(np.asarray(L0_ffn_rb), (128, E_R)))
    nc = build_nc()
    common = {
        "wg_s": np.ascontiguousarray(L0_ffn_Wg_s),
        "wu_s": np.ascontiguousarray(L0_ffn_Wu_s),
        "wd_s": np.ascontiguousarray(L0_ffn_Wd_s),
        "wg_r": np.ascontiguousarray(L0_ffn_Wg),
        "wu_r": np.ascontiguousarray(L0_ffn_Wu),
        "wd_r": np.ascontiguousarray(L0_ffn_Wd),
        "wr_t": wr_t,
        "rb_b": rb_b,
        "qproj": np.ascontiguousarray(L0_attn_q_proj),
        "kproj": np.ascontiguousarray(L0_attn_k_proj),
    }
    in_maps = [dict(common, x_in=np.ascontiguousarray(shards[c]))
               for c in range(N_CORES)]
    res = run_bass_kernel_spmd(nc, in_maps, core_ids=list(range(N_CORES)))
    out = np.stack([np.asarray(res.results[c]["x_out"]) for c in range(N_CORES)])
    return out.reshape(B, S, H).astype(np.float32)


# version 11
# speedup vs baseline: 1.0837x; 1.0837x over previous
import numpy as np
import ml_dtypes
import concourse.bass as bass
import concourse.mybir as mybir
from concourse.bass_utils import run_bass_kernel_spmd

F32 = mybir.dt.float32
BF16 = mybir.dt.bfloat16
AF = mybir.ActivationFunctionType
ALU = mybir.AluOpType
AX = mybir.AxisListType

H = 1024
I = 2048
E_R = 8
PROJ = 256
N_LT = 32
N_Q = N_LT // 8
N_CORES = 8
HC = H // 128
IC_H = 8
LN_EPS = 1e-5

ENGINES = ("tensor", "vector", "scalar", "sync", "gpsimd")


PARANOID_ALL = [False]


class Prog:

    N_DMA_POOL = 6

    def __init__(self, nc):
        self.nc = nc
        self.ops = {e: [] for e in ENGINES}
        self.sem_names = ["pe", "dve", "act", "cc"]
        self.dma_pools = {}
        for q in ("sp", "aq", "pq"):
            pool = [f"{q}{i}" for i in range(self.N_DMA_POOL)]
            self.dma_pools[q] = pool
            self.sem_names += pool
        self.dma_rr = {q: 0 for q in self.dma_pools}
        self.counts = {s: 0 for s in self.sem_names}
        self.bufs = {}
        self.last_wait = {}
        self.eng_sem = {"tensor": "pe", "vector": "dve", "scalar": "act",
                        "sync": None, "gpsimd": None}

    def _dep(self, key):
        return self.bufs.setdefault(key, {"w": {}, "r": {}})

    def alias(self, dst, srcs):
        d = self._dep(dst)
        for sk in srcs:
            e = self._dep(sk)
            for s, v in e["w"].items():
                d["w"][s] = max(d["w"].get(s, 0), v)
            for s, v in e["r"].items():
                d["r"][s] = max(d["r"].get(s, 0), v)

    def op(self, eng, fn, reads=(), writes=(), sem=None, inc=0,
           extra_waits=(), no_self_sync=False):
        waits = {}
        for k in reads:
            for s, v in self._dep(k)["w"].items():
                waits[s] = max(waits.get(s, 0), v)
        for k in writes:
            d = self._dep(k)
            for s, v in d["w"].items():
                waits[s] = max(waits.get(s, 0), v)
            for s, v in d["r"].items():
                waits[s] = max(waits.get(s, 0), v)
        for s, v in extra_waits:
            waits[s] = max(waits.get(s, 0), v)
        own = self.eng_sem[eng]
        if (no_self_sync or eng == "tensor") and own in waits:
            del waits[own]
        emit = []
        for s, v in sorted(waits.items()):
            lw = self.last_wait.get((eng, s), 0)
            if v > lw:
                emit.append((s, v))
                self.last_wait[(eng, s)] = v
        if inc:
            assert sem is not None
            self.counts[sem] += inc
            val = self.counts[sem]
        else:
            sem, val = None, None
        self.ops[eng].append((emit, fn, sem, inc))
        if inc:
            for k in reads:
                d = self._dep(k)
                d["r"][sem] = max(d["r"].get(sem, 0), val)
            for k in writes:
                d = self._dep(k)
                d["w"] = {sem: val}
                d["r"] = {}
        else:
            assert not writes, f"write without inc: {writes}"
        if PARANOID_ALL[0] and inc:
            self.barrier()
        return val

    def dma_group(self, eng, queue, items):
        pool = self.dma_pools[queue]
        sem = pool[self.dma_rr[queue] % len(pool)]
        self.dma_rr[queue] += 1
        prior_end = self.counts[sem]
        if prior_end > self.last_wait.get((eng, sem), 0):
            self.ops[eng].append(([(sem, prior_end)], None, None, 0))
            self.last_wait[(eng, sem)] = prior_end
        group_end = prior_end + 16 * len(items)
        item_waits = []
        for fn, reads, writes in items:
            waits = {}
            for k in reads:
                for s, v in self._dep(k)["w"].items():
                    waits[s] = max(waits.get(s, 0), v)
            for k in writes:
                d = self._dep(k)
                for s, v in d["w"].items():
                    waits[s] = max(waits.get(s, 0), v)
                for s, v in d["r"].items():
                    waits[s] = max(waits.get(s, 0), v)
            item_waits.append(waits)
        for (fn, reads, writes), waits in zip(items, item_waits):
            emit = []
            for s, v in sorted(waits.items()):
                lw = self.last_wait.get((eng, s), 0)
                if v > lw:
                    emit.append((s, v))
                    self.last_wait[(eng, s)] = v
            self.counts[sem] += 16
            self.ops[eng].append((emit, fn, sem, 16))
        for fn, reads, writes in items:
            for k in reads:
                d = self._dep(k)
                d["r"][sem] = max(d["r"].get(sem, 0), group_end)
            for k in writes:
                d = self._dep(k)
                d["w"] = {sem: group_end}
                d["r"] = {}
        assert self.counts[sem] == group_end
        if PARANOID_ALL[0]:
            self.barrier()
        return group_end

    def barrier(self):
        snap = [(s, v) for s, v in self.counts.items() if v > 0]
        for eng in ENGINES:
            emit = []
            for s, v in snap:
                if v > self.last_wait.get((eng, s), 0):
                    emit.append((s, v))
                    self.last_wait[(eng, s)] = v
            if emit:
                self.ops[eng].append((emit, None, None, 0))

    def waitpoint(self, eng, reads):
        waits = {}
        for k in reads:
            for s, v in self._dep(k)["w"].items():
                waits[s] = max(waits.get(s, 0), v)
        emit = []
        for s, v in sorted(waits.items()):
            lw = self.last_wait.get((eng, s), 0)
            if v > lw:
                emit.append((s, v))
                self.last_wait[(eng, s)] = v
        if emit:
            self.ops[eng].append((emit, None, None, 0))

    def replay(self, block, sems):
        def make_body(eng_name):
            def body(e):
                for waits, fn, sem, inc in self.ops[eng_name]:
                    for s, v in waits:
                        e.wait_ge(sems[s], v)
                    if fn is not None:
                        ins = fn(e)
                        if inc:
                            ins.then_inc(sems[sem], inc)
            return body
        block.tensor(make_body("tensor"))
        block.vector(make_body("vector"))
        block.scalar(make_body("scalar"))
        block.sync(make_body("sync"))
        block.gpsimd(make_body("gpsimd"))


def build_nc(n_lt=N_LT, n_routed=E_R, stages="faf", attn_debug=False, paranoid=""):
    paranoid = set(str(paranoid).split(",")) if paranoid else set()
    PARANOID_ALL[0] = ("all" in paranoid)

    class _Sect:
        def __init__(self, name):
            self.name = name
        def __enter__(self):
            self.prev = PARANOID_ALL[0]
            if self.name in paranoid:
                PARANOID_ALL[0] = True
        def __exit__(self, *a):
            PARANOID_ALL[0] = self.prev
            return False
    nc = bass.Bass("TRN2", target_bir_lowering=False,
                   dynamic_dma_scratch_size=512)
    NT = n_lt * 128
    n_q = n_lt // 8

    x_in = nc.dram_tensor("x_in", [NT, H], F32, kind="ExternalInput").ap()
    wg_s = nc.dram_tensor("wg_s", [I, H], F32, kind="ExternalInput").ap()
    wu_s = nc.dram_tensor("wu_s", [I, H], F32, kind="ExternalInput").ap()
    wd_s = nc.dram_tensor("wd_s", [H, I], F32, kind="ExternalInput").ap()
    wg_r = nc.dram_tensor("wg_r", [n_routed, I, H], F32, kind="ExternalInput").ap()
    wu_r = nc.dram_tensor("wu_r", [n_routed, I, H], F32, kind="ExternalInput").ap()
    wd_r = nc.dram_tensor("wd_r", [n_routed, H, I], F32, kind="ExternalInput").ap()
    wr_t = nc.dram_tensor("wr_t", [H, n_routed], F32, kind="ExternalInput").ap()
    rb_b = nc.dram_tensor("rb_b", [128, n_routed], F32, kind="ExternalInput").ap()
    qproj = nc.dram_tensor("qproj", [H, PROJ], F32, kind="ExternalInput").ap()
    kproj = nc.dram_tensor("kproj", [H, PROJ], F32, kind="ExternalInput").ap()
    x_out = nc.dram_tensor("x_out", [NT, H], F32, kind="ExternalOutput").ap()

    idb_dram = nc.inline_tensor(np.eye(128, dtype=ml_dtypes.bfloat16), "idb_dram").ap()
    idf_dram = nc.inline_tensor(np.eye(128, dtype=np.float32), "idf_dram").ap()

    agT_sc = nc.dram_tensor("agT_sc", [(n_routed + 1) * 2, 128, HC * 1024], BF16).ap()
    auT_sc = nc.dram_tensor("auT_sc", [(n_routed + 1) * 2, 128, HC * 1024], BF16).ap()
    bdT_sc = nc.dram_tensor("bdT_sc", [(n_routed + 1) * 2, 128, IC_H * 1024], BF16).ap()
    cc_in = nc.dram_tensor("cc_in", [2 * 128, H], F32).ap()
    cc_out = nc.dram_tensor("cc_out", [2 * 128, H], F32).ap()
    if attn_debug:
        d_x1 = nc.dram_tensor("d_x1", [NT, H], F32, kind="ExternalOutput").ap()
        d_kv = nc.dram_tensor("d_kv", [128, 2 * H], F32, kind="ExternalOutput").ap()
        d_kvpart = nc.dram_tensor("d_kvpart", [128, 2 * H], F32, kind="ExternalOutput").ap()
        d_qt = nc.dram_tensor("d_qt", [128, 2 * NT], BF16, kind="ExternalOutput").ap()
        d_z = nc.dram_tensor("d_z", [128, n_lt], F32, kind="ExternalOutput").ap()

    off = [(nc.sbuf_base + 31) // 32 * 32]

    def sb(name, cols, dtype, at=None):
        nbytes = cols * (2 if dtype == BF16 else 4)
        if at is None:
            at = off[0]
            off[0] += (nbytes + 31) // 32 * 32
        else:
            assert at + nbytes <= off[0] or True
        return nc.alloc_sbuf_tensor_at(name, [128, cols], dtype, offset=at), at

    X, _ = sb("X", n_lt * H, F32)
    A_g, A_g_at = sb("A_g", HC * 1024, BF16)
    A_u, A_u_at = sb("A_u", HC * 1024, BF16)
    B_d, B_d_at = sb("B_d", IC_H * 1024, BF16)
    C_h, C_h_at = sb("C_h", IC_H * 1024, BF16)
    XLT, _ = sb("XLT", HC * 1024, BF16)
    LNF, _ = sb("LNF", 1024, BF16)
    GS0, _ = sb("GS0", 512, BF16)
    CW, _ = sb("CW", n_lt * n_routed, F32)
    PR0, _ = sb("PR0", n_routed, F32)
    PR1, _ = sb("PR1", n_routed, F32)
    STT, _ = sb("STT", 32, F32)
    STS = [[STT[:, i * 4 + j:i * 4 + j + 1] for j in range(4)] for i in range(8)]
    M1, _ = sb("M1", 1, F32)
    M2, _ = sb("M2", 1, F32)
    EPSC, _ = sb("EPSC", 1, F32)
    N_CS = 3
    CSF = [sb(f"CSF{i}", 512, F32)[0] for i in range(N_CS)]
    CSB = [sb(f"CSB{i}", 512, BF16)[0] for i in range(2)]
    TB = [sb(f"TB{i}", 512, BF16)[0] for i in range(2)]
    IDF, _ = sb("IDF", 128, F32)
    IDB, _ = sb("IDB", 128, BF16)
    RBB, _ = sb("RBB", n_routed, F32)
    WRT, _ = sb("WRT", HC * n_routed, BF16)
    ZS = None
    QT, _ = sb("QT", 2 * NT, BF16, at=C_h_at)
    KVS, _ = sb("KVS", 2 * H, F32, at=A_g_at)
    KVB, _ = sb("KVB", 2 * H, BF16, at=A_g_at + 8192)
    QPF, _ = sb("QPF", PROJ, F32, at=A_u_at)
    KPF, _ = sb("KPF", PROJ, F32, at=A_u_at + 1024)
    KPB, _ = sb("KPB", PROJ, BF16, at=A_u_at + 2048)
    LN2B, _ = sb("LN2B", 1024, BF16, at=A_u_at + 2560)
    LN2T, _ = sb("LN2T", HC * 128, BF16, at=A_u_at + 4608)
    ZS, _ = sb("ZS", n_lt, F32, at=A_u_at + 8192)
    ZI = ZS
    LN2B2, _ = sb("LN2B2", 1024, BF16, at=A_u_at + 8704)
    LN2T2, _ = sb("LN2T2", HC * 128, BF16, at=A_u_at + 10752)
    QPF2, _ = sb("QPF2", PROJ, F32, at=A_u_at + 12800)
    KPF2, _ = sb("KPF2", PROJ, F32, at=A_u_at + 13824)
    KPB2, _ = sb("KPB2", PROJ, BF16, at=A_u_at + 14848)
    QPJ, _ = sb("QPJ", HC * PROJ, BF16, at=B_d_at)
    KPJ, _ = sb("KPJ", HC * PROJ, BF16, at=B_d_at + 4096)

    assert off[0] <= nc.sbuf_top, f"SBUF over budget: {off[0]} > {nc.sbuf_top}"
    import sys
    print(f"[kernel] sbuf used: {off[0]} of {nc.sbuf_top}", file=sys.stderr)
    P = [nc.alloc_psum_tensor(f"P{i}", [128, 512], F32) for i in range(8)]

    pr = Prog(nc)

    pr.op("vector", lambda v: v.memset(EPSC[:, :], LN_EPS),
          writes=["EPSC"], sem="dve", inc=1)
    pr.dma_group("sync", "sp", [
        (lambda s: s.dma_start(out=IDB[:, :], in_=idb_dram), [], ["IDB"]),
        (lambda s: s.dma_start(out=IDF[:, :], in_=idf_dram), [], ["IDF"]),
        (lambda s: s.dma_start(out=RBB[:, :], in_=rb_b), [], ["RBB"]),
    ])
    x_in3 = x_in.rearrange("(t p) d -> p t d", p=128)
    x_out3 = x_out.rearrange("(t p) d -> p t d", p=128)
    X3 = X[:, :].rearrange("p (t d) -> p t d", t=n_lt)
    pr.dma_group("sync", "sp", [
        (lambda s: s.dma_start(out=X3[:, 0:8, :], in_=x_in3[:, 0:8, :]),
         [], [f"X{lt}" for lt in range(8)])])
    for q in range(1, n_q):
        pr.dma_group("scalar", "aq", [
            (lambda s, q=q: s.dma_start(
                out=X3[:, q * 8:(q + 1) * 8, :],
                in_=x_in3[:, q * 8:(q + 1) * 8, :]),
             [], [f"X{lt}" for lt in range(q * 8, q * 8 + 8)])])
    cast_ci = [0]

    def cast_small(src3d, dst3d, dstkey, nchunk, per):
        for i in range(nchunk):
            b = cast_ci[0] % 2
            cast_ci[0] += 1
            fk, bk = f"CSF{b}", f"CSB{b}"
            pr.dma_group("sync", "sp", [
                (lambda s, i=i, b=b: s.dma_start(
                    out=CSF[b][:, 0:per], in_=src3d[:, i, :]),
                 [], [fk])])
            pr.op("scalar", lambda sc, b=b: sc.copy(
                out=CSB[b][:, 0:per], in_=CSF[b][:, 0:per]),
                reads=[fk], writes=[bk], sem="act", inc=1)
            pr.op("vector", lambda v, i=i, b=b: v.tensor_copy(
                out=dst3d[:, i, :], in_=CSB[b][:, 0:per]),
                reads=[bk], writes=[dstkey], sem="dve", inc=1)

    cast_small(wr_t.rearrange("(hc p) e -> p hc e", p=128),
               WRT[:, :].rearrange("p (hc e) -> p hc e", hc=HC),
               "WRT", HC, n_routed)

    p3b = P[3][:, :].bitcast(BF16)

    def prep_steps(e, ih):
        k = e * 2 + ih
        srcs = [(wg_s if e == 0 else wg_r[e - 1], agT_sc, f"agT{k}", True),
                (wu_s if e == 0 else wu_r[e - 1], auT_sc, f"auT{k}", True),
                (wd_s if e == 0 else wd_r[e - 1], bdT_sc, f"bdT{k}", False)]
        fronts, backs = [], []
        for src, dstT, dkey, is_gu in srcs:
            for rb in range(8):
                for cb in range(2):
                    ci = cast_ci[0]
                    cast_ci[0] += 1
                    b = ci % N_CS
                    tb = ci % 2
                    fk, bk, tk = f"CSF{b}", f"CSB{tb}", f"TB{tb}"
                    if is_gu:
                        r0, c0 = ih * 1024 + rb * 128, cb * 512
                    else:
                        r0, c0 = rb * 128, ih * 1024 + cb * 512

                    def front(src=src, b=b, tb=tb, fk=fk, bk=bk,
                              r0=r0, c0=c0):
                        pr.dma_group("sync", "sp", [
                            (lambda s: s.dma_start(
                                out=CSF[b][:, :], in_=src[r0:r0 + 128, c0:c0 + 512]),
                             [], [fk])])
                        pr.op("scalar", lambda sc: sc.copy(
                            out=CSB[tb][:, :], in_=CSF[b][:, :]),
                            reads=[fk], writes=[bk], sem="act", inc=1)

                    def back(dstT=dstT, dkey=dkey, is_gu=is_gu, rb=rb,
                             cb=cb, k=k, tb=tb, bk=bk, tk=tk, r0=r0):
                        def tr(t):
                            for tt in range(4):
                                ins = t.transpose(
                                    p3b[:, tt * 128:tt * 128 + 128],
                                    CSB[tb][:, tt * 128:tt * 128 + 128],
                                    IDB[:, :])
                            return ins
                        pr.op("tensor", tr, reads=[bk, "IDB", "P3"],
                              writes=["P3"], sem="pe", inc=1)
                        pr.op("vector", lambda v: v.tensor_copy(
                            out=TB[tb][:, :], in_=p3b[:, 0:512]),
                            reads=["P3"], writes=[tk], sem="dve", inc=1)
                        if is_gu:
                            col0 = (cb * 8 + rb) * 512
                            pr.dma_group("scalar", "aq", [
                                (lambda s: s.dma_start(
                                    out=dstT[k][:, col0:col0 + 512],
                                    in_=TB[tb][:, :]), [tk], [dkey])])
                        else:
                            bd3 = dstT[k].rearrange("p (ic o) -> p ic o", ic=IC_H)
                            pr.dma_group("scalar", "aq", [
                                (lambda s: s.dma_start(
                                    out=bd3[:, cb * 4:cb * 4 + 4, r0:r0 + 128],
                                    in_=TB[tb][:, :].rearrange(
                                        "p (t o) -> p t o", t=4)),
                                 [tk], [dkey])])
                    fronts.append(front)
                    backs.append(back)
        return fronts, backs

    prep_fronts, prep_backs = [], []
    pf_cnt, pb_cnt = [0], [0]

    def enqueue_prep(e, ih):
        fr, bk = prep_steps(e, ih)
        prep_fronts.extend(fr)
        prep_backs.extend(bk)

    def pump(n=10**9):
        while prep_backs and n > 0:
            while prep_fronts and pf_cnt[0] < pb_cnt[0] + 2:
                prep_fronts.pop(0)()
                pf_cnt[0] += 1
            prep_backs.pop(0)()
            pb_cnt[0] += 1
            n -= 1

    def ag_col(hc, ic):
        return ((hc // 4) * 8 + ic) * 512 + (hc % 4) * 128

    def load_AB(e, ih):
        k = e * 2 + ih
        pr.dma_group("sync", "sp", [
            (lambda s: s.dma_start(out=A_g[:, :], in_=agT_sc[k]),
             [f"agT{k}"], ["A_g"]),
            (lambda s: s.dma_start(out=A_u[:, :], in_=auT_sc[k]),
             [f"auT{k}"], ["A_u"]),
        ])
        pr.dma_group("sync", "sp", [
            (lambda s: s.dma_start(out=B_d[:, :], in_=bdT_sc[k]),
             [f"bdT{k}"], ["B_d"]),
        ])

    def ln_stats(lt, dump_lnf=False):
        Xk = f"X{lt}"
        Xsl = X[:, lt * H:(lt + 1) * H]
        ST0, ST1, ST2, ST3 = STS[lt % 8]
        s0, s1, s2, s3 = (f"ST{j}_{lt % 8}" for j in range(4))
        lt8 = lt % 8
        xlt3 = XLT[:, :].rearrange("p (hc t) -> p hc t", hc=HC)
        if dump_lnf:
            dump, dkey = LNF[:, :], "LNF"
        else:
            dump, dkey = xlt3[:, :, lt8 * 128:(lt8 + 1) * 128], f"XLT{lt8}"
        pr.op("vector", lambda v: v.reduce_sum(ST0, Xsl, axis=AX.X),
              reads=[Xk], writes=[s0], sem="dve", inc=1)
        pr.op("vector", lambda v: v.tensor_scalar(
            out=ST0, in0=ST0, scalar1=-1.0 / H, scalar2=None,
            op0=ALU.mult), reads=[s0], writes=[s0], sem="dve", inc=1)
        pr.op("scalar", lambda sc: sc.activation(
            out=dump, in_=Xsl, func=AF.Square, bias=ST0,
            accum_out=ST1),
            reads=[Xk, s0], writes=[dkey, s1], sem="act", inc=1)
        pr.op("scalar", lambda sc: sc.activation(
            out=ST2, in_=ST1, func=AF.Sqrt, scale=1.0 / H,
            bias=EPSC[:, :]), reads=[s1, "EPSC"], writes=[s2],
            sem="act", inc=1)
        pr.op("vector", lambda v: v.reciprocal(ST2, ST2),
              reads=[s2], writes=[s2], sem="dve", inc=1)
        pr.op("vector", lambda v: v.tensor_scalar(
            out=ST3, in0=ST0, scalar1=ST2, scalar2=None,
            op0=ALU.mult), reads=[s0, s2], writes=[s3], sem="dve", inc=1)

    def ln_apply(lt, with_ln2=False):
        Xk = f"X{lt}"
        Xsl = X[:, lt * H:(lt + 1) * H]
        ST0, ST1, ST2, ST3 = STS[lt % 8]
        s2, s3 = f"ST2_{lt % 8}", f"ST3_{lt % 8}"
        lt8 = lt % 8
        xlt3 = XLT[:, :].rearrange("p (hc t) -> p hc t", hc=HC)
        pr.op("scalar", lambda sc: sc.activation(
            out=LNF[:, :], in_=Xsl, func=AF.Identity, scale=ST2,
            bias=ST3),
            reads=[Xk, s2, s3], writes=["LNF"], sem="act", inc=1)
        bank = P[3] if lt % 2 == 0 else P[2]
        bkey = "P3" if lt % 2 == 0 else "P2"
        pb = bank[:, :].bitcast(BF16)
        def tr(t, pb=pb, lnf=LNF):
            for hc in range(HC):
                ins = t.transpose(pb[:, hc * 128:hc * 128 + 128],
                                  lnf[:, hc * 128:hc * 128 + 128], IDB[:, :])
            return ins
        pr.op("tensor", tr, reads=["LNF", "IDB", bkey],
              writes=[bkey], sem="pe", inc=1)
        if not with_ln2:
            pr.op("vector", lambda v, lt8=lt8, pb=pb: v.tensor_copy(
                out=xlt3[:, :, lt8 * 128:(lt8 + 1) * 128],
                in_=pb[:, :].rearrange("p (hc t) -> p hc t", hc=HC)),
                reads=[bkey], writes=[f"XLT{lt8}"], sem="dve", inc=1)
        else:
            par = lt % 2
            l2b = LN2B if par == 0 else LN2B2
            l2t = LN2T if par == 0 else LN2T2
            bk, tk2 = f"LN2B{par}", f"LN2T{par}"
            pr.op("scalar", lambda sc, l2b=l2b: sc.copy(out=l2b[:, :], in_=LNF[:, :]),
                  reads=["LNF"], writes=[bk], sem="act", inc=1)
            pr.op("vector", lambda v, l2t=l2t, pb=pb: v.tensor_copy(
                out=l2t[:, :].rearrange("p (hc t) -> p hc t", hc=HC),
                in_=pb[:, :].rearrange("p (hc t) -> p hc t", hc=HC)),
                reads=[bkey], writes=[tk2], sem="dve", inc=1)

    def ln_tile(lt, with_ln2=False):
        ln_stats(lt)
        ln_apply(lt, with_ln2)

    def routing(q):
        for lt8 in range(8):
            lt = q * 8 + lt8
            rp = P[5] if lt8 % 2 == 0 else P[6]
            rpk = "P5" if lt8 % 2 == 0 else "P6"
            def mm(t, lt8=lt8, rp=rp):
                for hc in range(HC):
                    ins = t.matmul(rp[:, 0:n_routed],
                                   XLT[:, hc * 1024 + lt8 * 128:hc * 1024 + lt8 * 128 + 128],
                                   WRT[:, hc * n_routed:(hc + 1) * n_routed],
                                   start=(hc == 0), stop=(hc == HC - 1))
                return ins
            pr.op("tensor", mm,
                  reads=["WRT", rpk, f"XLT{lt8}"], writes=[rpk],
                  sem="pe", inc=1)
            pr.op("vector", lambda v, rp=rp: v.tensor_tensor(
                out=PR0[:, :], in0=rp[:, 0:n_routed], in1=RBB[:, :], op=ALU.add),
                reads=[rpk, "RBB"], writes=["PR0"], sem="dve", inc=1)
            pr.op("scalar", lambda sc: sc.activation(
                out=PR0[:, :], in_=PR0[:, :], func=AF.Sigmoid),
                reads=["PR0"], writes=["PR0"], sem="act", inc=1)
            pr.op("vector", lambda v: v.reduce_max(M1[:, :], PR0[:, :], axis=AX.X),
                  reads=["PR0"], writes=["M1"], sem="dve", inc=1)
            pr.op("vector", lambda v: v.tensor_scalar(
                out=PR1[:, :], in0=PR0[:, :], scalar1=M1[:, :], scalar2=None,
                op0=ALU.is_ge), reads=["PR0", "M1"], writes=["PR1"],
                sem="dve", inc=1)
            pr.op("vector", lambda v: v.scalar_tensor_tensor(
                out=PR1[:, :], in0=PR1[:, :], scalar=-1e30, in1=PR0[:, :],
                op0=ALU.mult, op1=ALU.add),
                reads=["PR1", "PR0"], writes=["PR1"], sem="dve", inc=1)
            pr.op("vector", lambda v: v.reduce_max(M2[:, :], PR1[:, :], axis=AX.X),
                  reads=["PR1"], writes=["M2"], sem="dve", inc=1)
            pr.op("vector", lambda v: v.scalar_tensor_tensor(
                out=PR1[:, :], in0=PR0[:, :], scalar=M2[:, :], in1=PR0[:, :],
                op0=ALU.is_ge, op1=ALU.mult),
                reads=["PR0", "M2"], writes=["PR1"], sem="dve", inc=1)
            pr.op("vector", lambda v, lt=lt: v.tensor_scalar(
                out=CW[:, lt * n_routed:(lt + 1) * n_routed], in0=PR1[:, :],
                scalar1=0.5, scalar2=None, op0=ALU.mult),
                reads=["PR1"], writes=["CW"], sem="dve", inc=1)

    def load_A(e, ih):
        if (e, ih) in wb_done:
            k = e * 2 + ih
            pr.dma_group("sync", "sp", [
                (lambda s, k=k: s.dma_start(out=A_g[:, :], in_=agT_sc[k]),
                 [f"agT{k}"], ["A_g"]),
                (lambda s, k=k: s.dma_start(out=A_u[:, :], in_=auT_sc[k]),
                 [f"auT{k}"], ["A_u"]),
            ])
            return
        cast_expert(e)
        for la in (1, 2):
            if e + la <= n_routed:
                cast_expert(e + la)
        src_g = wgs_bf if e == 0 else wg_bf[e - 1]
        src_u = wus_bf if e == 0 else wu_bf[e - 1]
        items = []
        for hc in range(HC):
            items.append((lambda s, hc=hc, src=src_g, ih=ih: s.dma_start(
                out=A_g[:, hc * 1024:(hc + 1) * 1024],
                in_=src[ih * 1024:(ih + 1) * 1024, hc * 128:(hc + 1) * 128],
                transpose=True), [f"wbf_g{e}"], ["A_g"]))
        for hc in range(HC):
            items.append((lambda s, hc=hc, src=src_u, ih=ih: s.dma_start(
                out=A_u[:, hc * 1024:(hc + 1) * 1024],
                in_=src[ih * 1024:(ih + 1) * 1024, hc * 128:(hc + 1) * 128],
                transpose=True), [f"wbf_u{e}"], ["A_u"]))
        pr.dma_group("sync", "sp", items)
        k = e * 2 + ih
        pr.dma_group("scalar", "aq", [
            (lambda s, k=k: s.dma_start(out=agT_sc[k], in_=A_g[:, :]),
             ["A_g"], [f"agT{k}"]),
            (lambda s, k=k: s.dma_start(out=auT_sc[k], in_=A_u[:, :]),
             ["A_u"], [f"auT{k}"]),
        ])

    def load_B(e, ih):
        k = e * 2 + ih
        if (e, ih) in wb_done:
            pr.dma_group("sync", "sp", [
                (lambda s, k=k: s.dma_start(out=B_d[:, :], in_=bdT_sc[k]),
                 [f"bdT{k}"], ["B_d"])])
            return
        wb_done.add((e, ih))
        src_d = wds_bf if e == 0 else wd_bf[e - 1]
        items = []
        for ic in range(IC_H):
            col = (ih * IC_H + ic) * 128
            items.append((lambda s, ic=ic, col=col, src=src_d: s.dma_start(
                out=B_d[:, ic * 1024:(ic + 1) * 1024],
                in_=src[:, col:col + 128], transpose=True),
                [f"wbf_d{e}"], ["B_d"]))
        pr.dma_group("sync", "sp", items)
        pr.dma_group("scalar", "aq", [
            (lambda s, k=k: s.dma_start(out=bdT_sc[k], in_=B_d[:, :]),
             ["B_d"], [f"bdT{k}"])])

    def ffn_quarter_expert(q, e, ih, ln_queue=None):
        for ls2 in range(2):
            for ic in range(IC_H):
                pg = P[ic % 2]
                pgk = f"P{ic % 2}"
                gs = GS0
                gsk = "GS0"
                def gmm(t, pg=pg, ic=ic, ls2=ls2):
                    for hc in range(HC):
                        ins = t.matmul(
                            pg[:, :],
                            A_g[:, ag_col(hc, ic):ag_col(hc, ic) + 128],
                            XLT[:, hc * 1024 + ls2 * 512:hc * 1024 + ls2 * 512 + 512],
                            start=(hc == 0), stop=(hc == HC - 1))
                    return ins
                pr.op("tensor", gmm,
                      reads=["A_g", pgk] + [f"XLT{j}" for j in range(8)],
                      writes=[pgk], sem="pe", inc=1)
                pr.op("scalar", lambda sc, pg=pg, gs=gs: sc.activation(
                    out=gs[:, :], in_=pg[:, :], func=AF.Silu),
                    reads=[pgk], writes=[gsk], sem="act", inc=1)
                def umm(t, ic=ic, ls2=ls2):
                    for hc in range(HC):
                        ins = t.matmul(
                            P[2][:, :],
                            A_u[:, ag_col(hc, ic):ag_col(hc, ic) + 128],
                            XLT[:, hc * 1024 + ls2 * 512:hc * 1024 + ls2 * 512 + 512],
                            start=(hc == 0), stop=(hc == HC - 1))
                    return ins
                pr.op("tensor", umm,
                      reads=["A_u", "P2"] + [f"XLT{j}" for j in range(8)],
                      writes=["P2"], sem="pe", inc=1)
                pr.op("vector", lambda v, gs=gs, ic=ic, ls2=ls2:
                      v.tensor_tensor(
                          out=C_h[:, ic * 1024 + ls2 * 512:ic * 1024 + ls2 * 512 + 512],
                          in0=P[2][:, :], in1=gs[:, :], op=ALU.mult),
                      reads=["P2", gsk], writes=["C_h"], sem="dve", inc=1)
                pump(3)
        for sub in range(4):
            for lsub in range(2):
                pidx = sub * 2 + lsub
                base = 4 if pidx % 2 == 0 else 6
                def dmm(t, sub=sub, lsub=lsub, base=base):
                    for ic in range(IC_H):
                        for hoc in range(2):
                            ins = t.matmul(
                                P[base + hoc][:, :],
                                C_h[:, ic * 1024 + sub * 256 + lsub * 128:
                                    ic * 1024 + sub * 256 + lsub * 128 + 128],
                                B_d[:, ic * 1024 + hoc * 512:ic * 1024 + hoc * 512 + 512],
                                start=(ic == 0), stop=(ic == IC_H - 1),
                                skip_group_check=True)
                    return ins
                pkeys = [f"P{base}", f"P{base + 1}"]
                pr.op("tensor", dmm, reads=["C_h", "B_d"] + pkeys, writes=pkeys,
                      sem="pe", inc=1)
                lt = q * 8 + sub * 2 + lsub
                for hoc in range(2):
                    xsl = X[:, lt * H + hoc * 512:lt * H + hoc * 512 + 512]
                    if e == 0:
                        sca = 0.5
                        rd = []
                    else:
                        sca = CW[:, lt * n_routed + (e - 1):lt * n_routed + e]
                        rd = ["CW"]
                    pr.op("vector", lambda v, p=P[base + hoc], xsl=xsl, sca=sca:
                          v.scalar_tensor_tensor(out=xsl, in0=p[:, :], scalar=sca,
                                                 in1=xsl, op0=ALU.mult, op1=ALU.add),
                          reads=[f"P{base + hoc}", f"X{lt}"] + rd,
                          writes=[f"X{lt}"], sem="dve", inc=1)
                pump(3)
                if ln_queue:
                    ln_queue.pop(0)()

    prepped = set()

    def ffn_call(skip_ln_q0=False, store_out=False, tail_stats=None):
        phases = [(e, ih) for e in range(n_routed + 1) for ih in range(2)]
        for q in range(n_q):
            if q == 0 and not skip_ln_q0:
                with _Sect("ln"):
                    for lt8 in range(8):
                        ln_stats(lt8)
                    for lt8 in range(8):
                        ln_apply(lt8)
            with _Sect("routing"):
                routing(q)
            ln_sched = []
            for idx, (e, ih) in enumerate(phases):
                if idx == 0:
                    for jdx in (0, 1):
                        pe_, pih_ = phases[jdx]
                        if (pe_, pih_) not in prepped:
                            prepped.add((pe_, pih_))
                            enqueue_prep(pe_, pih_)
                            pump()
                    load_AB(e, ih)
                if idx + 2 < len(phases):
                    ne2 = phases[idx + 2]
                    if ne2 not in prepped:
                        prepped.add(ne2)
                        enqueue_prep(*ne2)
                if idx == len(phases) - 2:
                    if q + 1 < n_q:
                        ln_sched += [
                            (lambda lt=(q + 1) * 8 + j: ln_stats(lt, dump_lnf=True))
                            for j in range(8)]
                        ln_sched += [
                            (lambda lt=(q + 1) * 8 + j: ln_apply(lt))
                            for j in range(8)]
                    elif tail_stats:
                        ln_sched += [
                            (lambda j=j: ln_stats(j, dump_lnf=True))
                            for j in range(8)]
                with _Sect("expert"):
                    ffn_quarter_expert(q, e, ih, ln_queue=ln_sched)
                pump()
                if idx + 1 < len(phases):
                    load_AB(*phases[idx + 1])
                elif q + 1 < n_q:
                    load_AB(*phases[0])
            if store_out:
                pr.dma_group("sync", "sp", [
                    (lambda s, q=q: s.dma_start(
                        out=x_out3[:, q * 8:(q + 1) * 8, :],
                        in_=X3[:, q * 8:(q + 1) * 8, :]),
                     [f"X{lt}" for lt in range(q * 8, q * 8 + 8)],
                     ["x_out"])])

    def attention(stats_prefilled=False, post_ln_q0=False):
        if attn_debug:
            pr.dma_group("sync", "sp", [(lambda s: s.dma_start(
                out=d_x1.rearrange("(t p) d -> p t d", p=128),
                in_=X[:, :].rearrange("p (t d) -> p t d", t=n_lt)),
                [f"X{lt}" for lt in range(n_lt)], ["d_x1"])])
        pr.alias("QT", ["C_h"])
        pr.alias("KVS", ["A_g"])
        pr.alias("KVB", ["A_g"])
        for k in ("QPF", "KPF", "KPB", "LN2B", "LN2T", "ZS",
                  "QPF2", "KPF2", "KPB2", "LN2B2", "LN2T2"):
            pr.alias(k, ["A_u"])
        pr.alias("QPJ", ["B_d"])
        pr.alias("KPJ", ["B_d"])
        cast_small(qproj.rearrange("(hc p) e -> p hc e", p=128),
                   QPJ[:, :].rearrange("p (hc e) -> p hc e", hc=HC),
                   "QPJ", HC, PROJ)
        cast_small(kproj.rearrange("(hc p) e -> p hc e", p=128),
                   KPJ[:, :].rearrange("p (hc e) -> p hc e", hc=HC),
                   "KPJ", HC, PROJ)
        for lt in range(n_lt):
            par = lt % 2
            l2t, l2b = (LN2T, LN2B) if par == 0 else (LN2T2, LN2B2)
            qpf, kpf, kpb = (QPF, KPF, KPB) if par == 0 else (QPF2, KPF2, KPB2)
            tkq, tkk, tkb = f"QPF{par}", f"KPF{par}", f"KPB{par}"
            tk2, bk2 = f"LN2T{par}", f"LN2B{par}"
            gsl = GS0[:, par * 256:par * 256 + 256]
            gsk = f"GS0{par}"
            pq, pk = (P[0], P[1]) if par == 0 else (P[2], P[3])
            pqk, pkk = (("P0", "P1") if par == 0 else ("P2", "P3"))
            if lt == 0 and not stats_prefilled:
                for j in range(8):
                    ln_stats(j)
            ln_apply(lt, with_ln2=True)
            def qmm(t, l2t=l2t, pq=pq):
                for hc in range(HC):
                    ins = t.matmul(pq[:, 0:PROJ],
                                   l2t[:, hc * 128:(hc + 1) * 128],
                                   QPJ[:, hc * PROJ:(hc + 1) * PROJ],
                                   start=(hc == 0), stop=(hc == HC - 1),
                                   skip_group_check=True)
                return ins
            pr.op("tensor", qmm, reads=[tk2, "QPJ", pqk], writes=[pqk],
                  sem="pe", inc=1)
            pr.op("scalar", lambda sc, lt=lt, qpf=qpf, pq=pq: sc.activation(
                out=qpf[:, :], in_=pq[:, 0:PROJ], func=AF.Exp,
                accum_out=ZS[:, lt:lt + 1]),
                reads=[pqk], writes=[tkq, "ZS"], sem="act", inc=1)
            def kmm(t, l2t=l2t, pk=pk):
                for hc in range(HC):
                    ins = t.matmul(pk[:, 0:PROJ],
                                   l2t[:, hc * 128:(hc + 1) * 128],
                                   KPJ[:, hc * PROJ:(hc + 1) * PROJ],
                                   start=(hc == 0), stop=(hc == HC - 1),
                                   skip_group_check=True)
                return ins
            pr.op("tensor", kmm, reads=[tk2, "KPJ", pkk], writes=[pkk],
                  sem="pe", inc=1)
            pr.op("scalar", lambda sc, kpf=kpf, pk=pk: sc.activation(
                out=kpf[:, :], in_=pk[:, 0:PROJ], func=AF.Exp),
                reads=[pkk], writes=[tkk], sem="act", inc=1)
            pr.op("vector", lambda v, kpb=kpb, kpf=kpf: v.tensor_copy(
                out=kpb[:, :], in_=kpf[:, :]),
                reads=[tkk], writes=[tkb], sem="dve", inc=1)
            pr.op("vector", lambda v, gsl=gsl, qpf=qpf: v.tensor_copy(
                out=gsl, in_=qpf[:, :]),
                reads=[tkq], writes=[gsk], sem="dve", inc=1)
            p2b = (P[1] if par == 0 else P[0])[:, :].bitcast(BF16)
            trk = "P1" if par == 0 else "P0"
            def qtr(t, p2b=p2b, gsl=gsl):
                for ec in range(2):
                    ins = t.transpose(p2b[:, ec * 128:ec * 128 + 128],
                                      gsl[:, ec * 128:ec * 128 + 128], IDB[:, :])
                return ins
            pr.op("tensor", qtr, reads=[gsk, "IDB", trk], writes=[trk],
                  sem="pe", inc=1)
            pr.op("vector", lambda v, lt=lt, p2b=p2b: v.tensor_copy(
                out=QT[:, :].rearrange("p (ec t) -> p ec t", ec=2)
                    [:, :, lt * 128:(lt + 1) * 128],
                in_=p2b[:, 0:256].rearrange("p (ec t) -> p ec t", ec=2)),
                reads=[trk], writes=["QT"], sem="dve", inc=1)
            def kvmm(t, kpb=kpb, l2b=l2b):
                for ec in range(2):
                    for dsl in range(2):
                        ins = t.matmul(
                            P[4 + ec * 2 + dsl][:, :],
                            kpb[:, ec * 128:(ec + 1) * 128],
                            l2b[:, dsl * 512:(dsl + 1) * 512],
                            start=True, stop=True,
                            skip_group_check=True)
                return ins
            pr.op("tensor", kvmm,
                  reads=[tkb, bk2, "P4", "P5", "P6", "P7"],
                  writes=["P4", "P5", "P6", "P7"], sem="pe", inc=1)
            for ec in range(2):
                for dsl in range(2):
                    ksl = KVS[:, ec * 1024 + dsl * 512:ec * 1024 + dsl * 512 + 512]
                    if lt == 0:
                        pr.op("vector", lambda v, ec=ec, dsl=dsl, ksl=ksl:
                              v.tensor_copy(out=ksl, in_=P[4 + ec * 2 + dsl][:, :]),
                              reads=[f"P{4 + ec * 2 + dsl}"], writes=["KVS"],
                              sem="dve", inc=1)
                    else:
                        pr.op("vector", lambda v, ec=ec, dsl=dsl, ksl=ksl:
                              v.tensor_tensor(out=ksl, in0=P[4 + ec * 2 + dsl][:, :],
                                              in1=ksl, op=ALU.add),
                              reads=[f"P{4 + ec * 2 + dsl}", "KVS"], writes=["KVS"],
                              sem="dve", inc=1)
            if lt + 8 < n_lt:
                ln_stats(lt + 8)
        pr.op("vector", lambda v: v.tensor_scalar(
            out=ZI[:, :], in0=ZS[:, :], scalar1=1e-8, scalar2=None, op0=ALU.add),
            reads=["ZS"], writes=["ZI"], sem="dve", inc=1)
        pr.op("vector", lambda v: v.reciprocal(ZI[:, :], ZI[:, :]),
              reads=["ZI"], writes=["ZI"], sem="dve", inc=1)
        if attn_debug:
            pr.dma_group("sync", "sp", [(lambda s: s.dma_start(
                out=d_kvpart, in_=KVS[:, :]), ["KVS"], ["d_kvpart"])])
        pr.dma_group("sync", "sp", [(lambda s: s.dma_start(
            out=cc_in.rearrange("(ec p) d -> p ec d", p=128),
            in_=KVS[:, :].rearrange("p (ec d) -> p ec d", ec=2)),
            ["KVS"], ["cc_in"])])
        pr.op("gpsimd", lambda g: g.collective_compute(
            "AllReduce", ALU.add,
            replica_groups=[[0, 1, 2, 3], [4, 5, 6, 7]],
            ins=[cc_in], outs=[cc_out]),
            reads=["cc_in"], writes=["cc_out"], sem="cc", inc=1)
        pr.dma_group("sync", "sp", [(lambda s: s.dma_start(
            out=KVS[:, :].rearrange("p (ec d) -> p ec d", ec=2),
            in_=cc_out.rearrange("(ec p) d -> p ec d", p=128)),
            ["cc_out"], ["KVS"])])
        pr.op("vector", lambda v: v.tensor_copy(out=KVB[:, :], in_=KVS[:, :]),
              reads=["KVS"], writes=["KVB"], sem="dve", inc=1)
        if attn_debug:
            pr.dma_group("sync", "sp", [
                (lambda s: s.dma_start(out=d_kv, in_=KVS[:, :]), ["KVS"], ["d_kv"]),
                (lambda s: s.dma_start(out=d_qt, in_=QT[:, :]), ["QT"], ["d_qt"]),
                (lambda s: s.dma_start(out=d_z, in_=ZS[:, :]), ["ZS"], ["d_z"]),
            ])
        for lt in range(n_lt):
            for dsl in range(2):
                pb = (lt % 2) * 2 + dsl
                def nmm(t, lt=lt, dsl=dsl, pb=pb):
                    for ec in range(2):
                        ins = t.matmul(
                            P[pb][:, :],
                            QT[:, ec * NT + lt * 128:ec * NT + lt * 128 + 128],
                            KVB[:, ec * 1024 + dsl * 512:ec * 1024 + dsl * 512 + 512],
                            start=(ec == 0), stop=(ec == 1),
                            skip_group_check=True)
                    return ins
                pr.op("tensor", nmm, reads=["QT", "KVB", f"P{pb}"],
                      writes=[f"P{pb}"], sem="pe", inc=1)
                xsl = X[:, lt * H + dsl * 512:lt * H + dsl * 512 + 512]
                pr.op("vector", lambda v, p=P[pb], xsl=xsl, lt=lt:
                      v.scalar_tensor_tensor(
                          out=xsl, in0=p[:, :], scalar=ZI[:, lt:lt + 1],
                          in1=xsl, op0=ALU.mult, op1=ALU.add),
                      reads=[f"P{pb}", f"X{lt}", "ZI"], writes=[f"X{lt}"],
                      sem="dve", inc=1)
            if post_ln_q0:
                if lt < 8:
                    ln_stats(lt)
                if 3 <= lt < 11:
                    ln_apply(lt - 3)

    n_st = len(stages)
    for i, st in enumerate(stages):
        if st == "f":
            pr.alias("C_h", ["QT"])
            pr.alias("A_g", ["KVS", "KVB"])
            pr.alias("A_u", ["QPF", "KPF", "KPB", "LN2B", "LN2T", "ZS",
                     "QPF2", "KPF2", "KPB2", "LN2B2", "LN2T2"])
            pr.alias("B_d", ["QPJ", "KPJ"])
            ffn_call(skip_ln_q0=(i > 0 and stages[i - 1] == "a"),
                     store_out=(i == n_st - 1),
                     tail_stats=(i + 1 < n_st and stages[i + 1] == "a"))
        elif st == "a":
            attention(stats_prefilled=(i > 0 and stages[i - 1] == "f"),
                      post_ln_q0=(i + 1 < n_st and stages[i + 1] == "f"))

    if not (stages and stages[-1] == "f"):
        pr.dma_group("sync", "sp", [(lambda s: s.dma_start(
            out=x_out3, in_=X[:, :].rearrange("p (t d) -> p t d", t=n_lt)),
            [f"X{lt}" for lt in range(n_lt)], ["x_out"])])
    pr.waitpoint("sync", ["x_out"])

    sem_handles = {}
    ctxs = []
    for s in pr.sem_names:
        cm = nc.semaphore(s)
        sem_handles[s] = cm.__enter__()
        ctxs.append(cm)
    blk = nc.Block()
    block = blk.__enter__()
    pr.replay(block, sem_handles)
    blk.__exit__(None, None, None)
    for cm in reversed(ctxs):
        cm.__exit__(None, None, None)
    return nc


def kernel(L0_attn_x, L0_attn_q_proj, L0_attn_k_proj, L0_ffn_Wg_s, L0_ffn_Wu_s,
           L0_ffn_Wd_s, L0_ffn_Wg, L0_ffn_Wu, L0_ffn_Wd, L0_ffn_Wr, L0_ffn_rb):
    x = np.ascontiguousarray(np.asarray(L0_attn_x))
    B, S, _ = x.shape
    shards = x.reshape(B * 4, S // 4, H)
    wr_t = np.ascontiguousarray(np.asarray(L0_ffn_Wr).T)
    rb_b = np.ascontiguousarray(
        np.broadcast_to(np.asarray(L0_ffn_rb), (128, E_R)))
    nc = build_nc()
    common = {
        "wg_s": np.ascontiguousarray(L0_ffn_Wg_s),
        "wu_s": np.ascontiguousarray(L0_ffn_Wu_s),
        "wd_s": np.ascontiguousarray(L0_ffn_Wd_s),
        "wg_r": np.ascontiguousarray(L0_ffn_Wg),
        "wu_r": np.ascontiguousarray(L0_ffn_Wu),
        "wd_r": np.ascontiguousarray(L0_ffn_Wd),
        "wr_t": wr_t,
        "rb_b": rb_b,
        "qproj": np.ascontiguousarray(L0_attn_q_proj),
        "kproj": np.ascontiguousarray(L0_attn_k_proj),
    }
    in_maps = [dict(common, x_in=np.ascontiguousarray(shards[c]))
               for c in range(N_CORES)]
    res = run_bass_kernel_spmd(nc, in_maps, core_ids=list(range(N_CORES)))
    out = np.stack([np.asarray(res.results[c]["x_out"]) for c in range(N_CORES)])
    return out.reshape(B, S, H).astype(np.float32)


# version 13
# speedup vs baseline: 1.0880x; 1.0039x over previous
import numpy as np
import ml_dtypes
import concourse.bass as bass
import concourse.mybir as mybir
from concourse.bass_utils import run_bass_kernel_spmd

F32 = mybir.dt.float32
BF16 = mybir.dt.bfloat16
AF = mybir.ActivationFunctionType
ALU = mybir.AluOpType
AX = mybir.AxisListType

H = 1024
I = 2048
E_R = 8
PROJ = 256
N_LT = 32
N_Q = N_LT // 8
N_CORES = 8
HC = H // 128
IC_H = 8
LN_EPS = 1e-5

ENGINES = ("tensor", "vector", "scalar", "sync", "gpsimd")


PARANOID_ALL = [False]


class Prog:

    N_DMA_POOL = 6

    def __init__(self, nc):
        self.nc = nc
        self.ops = {e: [] for e in ENGINES}
        self.sem_names = ["pe", "dve", "act", "cc"]
        self.dma_pools = {}
        for q in ("sp", "aq", "pq"):
            pool = [f"{q}{i}" for i in range(self.N_DMA_POOL)]
            self.dma_pools[q] = pool
            self.sem_names += pool
        self.dma_rr = {q: 0 for q in self.dma_pools}
        self.counts = {s: 0 for s in self.sem_names}
        self.bufs = {}
        self.last_wait = {}
        self.eng_sem = {"tensor": "pe", "vector": "dve", "scalar": "act",
                        "sync": None, "gpsimd": None}

    def _dep(self, key):
        return self.bufs.setdefault(key, {"w": {}, "r": {}})

    def alias(self, dst, srcs):
        d = self._dep(dst)
        for sk in srcs:
            e = self._dep(sk)
            for s, v in e["w"].items():
                d["w"][s] = max(d["w"].get(s, 0), v)
            for s, v in e["r"].items():
                d["r"][s] = max(d["r"].get(s, 0), v)

    def op(self, eng, fn, reads=(), writes=(), sem=None, inc=0,
           extra_waits=(), no_self_sync=False):
        waits = {}
        for k in reads:
            for s, v in self._dep(k)["w"].items():
                waits[s] = max(waits.get(s, 0), v)
        for k in writes:
            d = self._dep(k)
            for s, v in d["w"].items():
                waits[s] = max(waits.get(s, 0), v)
            for s, v in d["r"].items():
                waits[s] = max(waits.get(s, 0), v)
        for s, v in extra_waits:
            waits[s] = max(waits.get(s, 0), v)
        own = self.eng_sem[eng]
        if (no_self_sync or eng == "tensor") and own in waits:
            del waits[own]
        emit = []
        for s, v in sorted(waits.items()):
            lw = self.last_wait.get((eng, s), 0)
            if v > lw:
                emit.append((s, v))
                self.last_wait[(eng, s)] = v
        if inc:
            assert sem is not None
            self.counts[sem] += inc
            val = self.counts[sem]
        else:
            sem, val = None, None
        self.ops[eng].append((emit, fn, sem, inc))
        if inc:
            for k in reads:
                d = self._dep(k)
                d["r"][sem] = max(d["r"].get(sem, 0), val)
            for k in writes:
                d = self._dep(k)
                d["w"] = {sem: val}
                d["r"] = {}
        else:
            assert not writes, f"write without inc: {writes}"
        if PARANOID_ALL[0] and inc:
            self.barrier()
        return val

    def dma_group(self, eng, queue, items):
        pool = self.dma_pools[queue]
        sem = pool[self.dma_rr[queue] % len(pool)]
        self.dma_rr[queue] += 1
        prior_end = self.counts[sem]
        if prior_end > self.last_wait.get((eng, sem), 0):
            self.ops[eng].append(([(sem, prior_end)], None, None, 0))
            self.last_wait[(eng, sem)] = prior_end
        group_end = prior_end + 16 * len(items)
        item_waits = []
        for fn, reads, writes in items:
            waits = {}
            for k in reads:
                for s, v in self._dep(k)["w"].items():
                    waits[s] = max(waits.get(s, 0), v)
            for k in writes:
                d = self._dep(k)
                for s, v in d["w"].items():
                    waits[s] = max(waits.get(s, 0), v)
                for s, v in d["r"].items():
                    waits[s] = max(waits.get(s, 0), v)
            item_waits.append(waits)
        for (fn, reads, writes), waits in zip(items, item_waits):
            emit = []
            for s, v in sorted(waits.items()):
                lw = self.last_wait.get((eng, s), 0)
                if v > lw:
                    emit.append((s, v))
                    self.last_wait[(eng, s)] = v
            self.counts[sem] += 16
            self.ops[eng].append((emit, fn, sem, 16))
        for fn, reads, writes in items:
            for k in reads:
                d = self._dep(k)
                d["r"][sem] = max(d["r"].get(sem, 0), group_end)
            for k in writes:
                d = self._dep(k)
                d["w"] = {sem: group_end}
                d["r"] = {}
        assert self.counts[sem] == group_end
        if PARANOID_ALL[0]:
            self.barrier()
        return group_end

    def barrier(self):
        snap = [(s, v) for s, v in self.counts.items() if v > 0]
        for eng in ENGINES:
            emit = []
            for s, v in snap:
                if v > self.last_wait.get((eng, s), 0):
                    emit.append((s, v))
                    self.last_wait[(eng, s)] = v
            if emit:
                self.ops[eng].append((emit, None, None, 0))

    def waitpoint(self, eng, reads):
        waits = {}
        for k in reads:
            for s, v in self._dep(k)["w"].items():
                waits[s] = max(waits.get(s, 0), v)
        emit = []
        for s, v in sorted(waits.items()):
            lw = self.last_wait.get((eng, s), 0)
            if v > lw:
                emit.append((s, v))
                self.last_wait[(eng, s)] = v
        if emit:
            self.ops[eng].append((emit, None, None, 0))

    def replay(self, block, sems):
        def make_body(eng_name):
            def body(e):
                for waits, fn, sem, inc in self.ops[eng_name]:
                    for s, v in waits:
                        e.wait_ge(sems[s], v)
                    if fn is not None:
                        ins = fn(e)
                        if inc:
                            ins.then_inc(sems[sem], inc)
            return body
        block.tensor(make_body("tensor"))
        block.vector(make_body("vector"))
        block.scalar(make_body("scalar"))
        block.sync(make_body("sync"))
        block.gpsimd(make_body("gpsimd"))


def build_nc(n_lt=N_LT, n_routed=E_R, stages="faf", attn_debug=False, paranoid=""):
    paranoid = set(str(paranoid).split(",")) if paranoid else set()
    PARANOID_ALL[0] = ("all" in paranoid)

    class _Sect:
        def __init__(self, name):
            self.name = name
        def __enter__(self):
            self.prev = PARANOID_ALL[0]
            if self.name in paranoid:
                PARANOID_ALL[0] = True
        def __exit__(self, *a):
            PARANOID_ALL[0] = self.prev
            return False
    nc = bass.Bass("TRN2", target_bir_lowering=False,
                   dynamic_dma_scratch_size=512)
    NT = n_lt * 128
    n_q = n_lt // 8

    x_in = nc.dram_tensor("x_in", [NT, H], F32, kind="ExternalInput").ap()
    wg_s = nc.dram_tensor("wg_s", [I, H], F32, kind="ExternalInput").ap()
    wu_s = nc.dram_tensor("wu_s", [I, H], F32, kind="ExternalInput").ap()
    wd_s = nc.dram_tensor("wd_s", [H, I], F32, kind="ExternalInput").ap()
    wg_r = nc.dram_tensor("wg_r", [n_routed, I, H], F32, kind="ExternalInput").ap()
    wu_r = nc.dram_tensor("wu_r", [n_routed, I, H], F32, kind="ExternalInput").ap()
    wd_r = nc.dram_tensor("wd_r", [n_routed, H, I], F32, kind="ExternalInput").ap()
    wr_t = nc.dram_tensor("wr_t", [H, n_routed], F32, kind="ExternalInput").ap()
    rb_b = nc.dram_tensor("rb_b", [128, n_routed], F32, kind="ExternalInput").ap()
    qproj = nc.dram_tensor("qproj", [H, PROJ], F32, kind="ExternalInput").ap()
    kproj = nc.dram_tensor("kproj", [H, PROJ], F32, kind="ExternalInput").ap()
    x_out = nc.dram_tensor("x_out", [NT, H], F32, kind="ExternalOutput").ap()

    idb_dram = nc.inline_tensor(np.eye(128, dtype=ml_dtypes.bfloat16), "idb_dram").ap()
    idf_dram = nc.inline_tensor(np.eye(128, dtype=np.float32), "idf_dram").ap()

    agT_sc = nc.dram_tensor("agT_sc", [(n_routed + 1) * 2, 128, HC * 1024], BF16).ap()
    auT_sc = nc.dram_tensor("auT_sc", [(n_routed + 1) * 2, 128, HC * 1024], BF16).ap()
    bdT_sc = nc.dram_tensor("bdT_sc", [(n_routed + 1) * 2, 128, IC_H * 1024], BF16).ap()
    cc_in = nc.dram_tensor("cc_in", [2 * 128, H], F32).ap()
    cc_out = nc.dram_tensor("cc_out", [2 * 128, H], F32).ap()
    if attn_debug:
        d_x1 = nc.dram_tensor("d_x1", [NT, H], F32, kind="ExternalOutput").ap()
        d_kv = nc.dram_tensor("d_kv", [128, 2 * H], F32, kind="ExternalOutput").ap()
        d_kvpart = nc.dram_tensor("d_kvpart", [128, 2 * H], F32, kind="ExternalOutput").ap()
        d_qt = nc.dram_tensor("d_qt", [128, 2 * NT], BF16, kind="ExternalOutput").ap()
        d_z = nc.dram_tensor("d_z", [128, n_lt], F32, kind="ExternalOutput").ap()

    off = [(nc.sbuf_base + 31) // 32 * 32]

    def sb(name, cols, dtype, at=None):
        nbytes = cols * (2 if dtype == BF16 else 4)
        if at is None:
            at = off[0]
            off[0] += (nbytes + 31) // 32 * 32
        else:
            assert at + nbytes <= off[0] or True
        return nc.alloc_sbuf_tensor_at(name, [128, cols], dtype, offset=at), at

    X, _ = sb("X", n_lt * H, F32)
    A_g, A_g_at = sb("A_g", HC * 1024, BF16)
    A_u, A_u_at = sb("A_u", HC * 1024, BF16)
    B_d, B_d_at = sb("B_d", IC_H * 1024, BF16)
    C_h, C_h_at = sb("C_h", IC_H * 1024, BF16)
    XLT, _ = sb("XLT", HC * 1024, BF16)
    LNF, _ = sb("LNF", 1024, BF16)
    GS0, _ = sb("GS0", 512, BF16)
    CW, _ = sb("CW", n_lt * n_routed, F32)
    PR0, _ = sb("PR0", n_routed, F32)
    PR1, _ = sb("PR1", n_routed, F32)
    STT, _ = sb("STT", 32, F32)
    STS = [[STT[:, i * 4 + j:i * 4 + j + 1] for j in range(4)] for i in range(8)]
    M1, _ = sb("M1", 1, F32)
    M2, _ = sb("M2", 1, F32)
    EPSC, _ = sb("EPSC", 1, F32)
    N_CS = 3
    CSF = [sb(f"CSF{i}", 512, F32)[0] for i in range(N_CS)]
    CSB = [sb(f"CSB{i}", 512, BF16)[0] for i in range(2)]
    TB = [sb(f"TB{i}", 512, BF16)[0] for i in range(2)]
    IDF, _ = sb("IDF", 128, F32)
    IDB, _ = sb("IDB", 128, BF16)
    RBB, _ = sb("RBB", n_routed, F32)
    WRT, _ = sb("WRT", HC * n_routed, BF16)
    ZS = None
    QT, _ = sb("QT", 2 * NT, BF16, at=C_h_at)
    KVS, _ = sb("KVS", 2 * H, F32, at=A_g_at)
    KVB, _ = sb("KVB", 2 * H, BF16, at=A_g_at + 8192)
    QPF, _ = sb("QPF", PROJ, F32, at=A_u_at)
    KPF, _ = sb("KPF", PROJ, F32, at=A_u_at + 1024)
    KPB, _ = sb("KPB", PROJ, BF16, at=A_u_at + 2048)
    LN2B, _ = sb("LN2B", 1024, BF16, at=A_u_at + 2560)
    LN2T, _ = sb("LN2T", HC * 128, BF16, at=A_u_at + 4608)
    ZS, _ = sb("ZS", n_lt, F32, at=A_u_at + 8192)
    ZI = ZS
    LN2B2, _ = sb("LN2B2", 1024, BF16, at=A_u_at + 8704)
    LN2T2, _ = sb("LN2T2", HC * 128, BF16, at=A_u_at + 10752)
    QPF2, _ = sb("QPF2", PROJ, F32, at=A_u_at + 12800)
    KPF2, _ = sb("KPF2", PROJ, F32, at=A_u_at + 13824)
    KPB2, _ = sb("KPB2", PROJ, BF16, at=A_u_at + 14848)
    QPJ, _ = sb("QPJ", HC * PROJ, BF16, at=B_d_at)
    KPJ, _ = sb("KPJ", HC * PROJ, BF16, at=B_d_at + 4096)

    assert off[0] <= nc.sbuf_top, f"SBUF over budget: {off[0]} > {nc.sbuf_top}"
    import sys
    print(f"[kernel] sbuf used: {off[0]} of {nc.sbuf_top}", file=sys.stderr)
    P = [nc.alloc_psum_tensor(f"P{i}", [128, 512], F32) for i in range(8)]

    pr = Prog(nc)

    pr.op("vector", lambda v: v.memset(EPSC[:, :], LN_EPS),
          writes=["EPSC"], sem="dve", inc=1)
    pr.dma_group("sync", "sp", [
        (lambda s: s.dma_start(out=IDB[:, :], in_=idb_dram), [], ["IDB"]),
        (lambda s: s.dma_start(out=IDF[:, :], in_=idf_dram), [], ["IDF"]),
        (lambda s: s.dma_start(out=RBB[:, :], in_=rb_b), [], ["RBB"]),
    ])
    x_in3 = x_in.rearrange("(t p) d -> p t d", p=128)
    x_out3 = x_out.rearrange("(t p) d -> p t d", p=128)
    X3 = X[:, :].rearrange("p (t d) -> p t d", t=n_lt)
    pr.dma_group("sync", "sp", [
        (lambda s: s.dma_start(out=X3[:, 0:8, :], in_=x_in3[:, 0:8, :]),
         [], [f"X{lt}" for lt in range(8)])])
    for q in range(1, n_q):
        pr.dma_group("scalar", "aq", [
            (lambda s, q=q: s.dma_start(
                out=X3[:, q * 8:(q + 1) * 8, :],
                in_=x_in3[:, q * 8:(q + 1) * 8, :]),
             [], [f"X{lt}" for lt in range(q * 8, q * 8 + 8)])])
    cast_ci = [0]

    def cast_small(src3d, dst3d, dstkey, nchunk, per):
        for i in range(nchunk):
            b = cast_ci[0] % 2
            cast_ci[0] += 1
            fk, bk = f"CSF{b}", f"CSB{b}"
            pr.dma_group("sync", "sp", [
                (lambda s, i=i, b=b: s.dma_start(
                    out=CSF[b][:, 0:per], in_=src3d[:, i, :]),
                 [], [fk])])
            pr.op("scalar", lambda sc, b=b: sc.copy(
                out=CSB[b][:, 0:per], in_=CSF[b][:, 0:per]),
                reads=[fk], writes=[bk], sem="act", inc=1)
            pr.op("vector", lambda v, i=i, b=b: v.tensor_copy(
                out=dst3d[:, i, :], in_=CSB[b][:, 0:per]),
                reads=[bk], writes=[dstkey], sem="dve", inc=1)

    cast_small(wr_t.rearrange("(hc p) e -> p hc e", p=128),
               WRT[:, :].rearrange("p (hc e) -> p hc e", hc=HC),
               "WRT", HC, n_routed)

    p3b = P[3][:, :].bitcast(BF16)

    PREP_PB = [(4, 5)]

    def prep_steps(e, ih, direct=False):
        k = e * 2 + ih
        srcs = [(wg_s if e == 0 else wg_r[e - 1], agT_sc, f"agT{k}", True),
                (wu_s if e == 0 else wu_r[e - 1], auT_sc, f"auT{k}", True),
                (wd_s if e == 0 else wd_r[e - 1], bdT_sc, f"bdT{k}", False)]
        fronts, backs = [], []
        for src, dstT, dkey, is_gu in srcs:
            for rb in range(8):
                for cb in range(2):
                    ci = cast_ci[0]
                    cast_ci[0] += 1
                    b = ci % N_CS
                    tb = ci % 2
                    fk, bk, tk = f"CSF{b}", f"CSB{tb}", f"TB{tb}"
                    if is_gu:
                        r0, c0 = ih * 1024 + rb * 128, cb * 512
                    else:
                        r0, c0 = rb * 128, ih * 1024 + cb * 512

                    def front(src=src, b=b, tb=tb, fk=fk, bk=bk,
                              r0=r0, c0=c0):
                        pr.dma_group("sync", "sp", [
                            (lambda s: s.dma_start(
                                out=CSF[b][:, :], in_=src[r0:r0 + 128, c0:c0 + 512]),
                             [], [fk])])
                        pr.op("scalar", lambda sc: sc.copy(
                            out=CSB[tb][:, :], in_=CSF[b][:, :]),
                            reads=[fk], writes=[bk], sem="act", inc=1)

                    def back(dstT=dstT, dkey=dkey, is_gu=is_gu, rb=rb,
                             cb=cb, k=k, tb=tb, bk=bk, tk=tk, r0=r0,
                             direct=direct):
                        pi = PREP_PB[0][pb_cnt[0] % 2]
                        pkey = f"P{pi}"
                        pib = P[pi][:, :].bitcast(BF16)
                        def tr(t, pib=pib):
                            for tt in range(4):
                                ins = t.transpose(
                                    pib[:, tt * 128:tt * 128 + 128],
                                    CSB[tb][:, tt * 128:tt * 128 + 128],
                                    IDB[:, :])
                            return ins
                        pr.op("tensor", tr, reads=[bk, "IDB", pkey],
                              writes=[pkey], sem="pe", inc=1)
                        if is_gu:
                            col0 = (cb * 8 + rb) * 512
                            if direct:
                                dsb, dsk = ((A_g, "A_g") if dkey[:3] == "agT"
                                            else (A_u, "A_u"))
                                sl = dsb[:, col0:col0 + 512]
                                pr.op("vector", lambda v, pib=pib, sl=sl:
                                      v.tensor_copy(out=sl, in_=pib[:, 0:512]),
                                      reads=[pkey], writes=[dsk],
                                      sem="dve", inc=1)
                                pr.dma_group("scalar", "aq", [
                                    (lambda s, sl=sl: s.dma_start(
                                        out=dstT[k][:, col0:col0 + 512],
                                        in_=sl), [dsk], [dkey])])
                            else:
                                pr.op("vector", lambda v, pib=pib: v.tensor_copy(
                                    out=TB[tb][:, :], in_=pib[:, 0:512]),
                                    reads=[pkey], writes=[tk], sem="dve", inc=1)
                                pr.dma_group("scalar", "aq", [
                                    (lambda s: s.dma_start(
                                        out=dstT[k][:, col0:col0 + 512],
                                        in_=TB[tb][:, :]), [tk], [dkey])])
                        else:
                            bd3 = dstT[k].rearrange("p (ic o) -> p ic o", ic=IC_H)
                            if direct:
                                Bd3 = B_d[:, :].rearrange(
                                    "p (ic o) -> p ic o", ic=IC_H)
                                sl = Bd3[:, cb * 4:cb * 4 + 4, r0:r0 + 128]
                                pr.op("vector", lambda v, pib=pib, sl=sl:
                                      v.tensor_copy(
                                          out=sl,
                                          in_=pib[:, 0:512].rearrange(
                                              "p (t o) -> p t o", t=4)),
                                      reads=[pkey], writes=["B_d"],
                                      sem="dve", inc=1)
                                pr.dma_group("scalar", "aq", [
                                    (lambda s, sl=sl: s.dma_start(
                                        out=bd3[:, cb * 4:cb * 4 + 4, r0:r0 + 128],
                                        in_=sl), ["B_d"], [dkey])])
                            else:
                                pr.op("vector", lambda v, pib=pib: v.tensor_copy(
                                    out=TB[tb][:, :], in_=pib[:, 0:512]),
                                    reads=[pkey], writes=[tk], sem="dve", inc=1)
                                pr.dma_group("scalar", "aq", [
                                    (lambda s: s.dma_start(
                                        out=bd3[:, cb * 4:cb * 4 + 4, r0:r0 + 128],
                                        in_=TB[tb][:, :].rearrange(
                                            "p (t o) -> p t o", t=4)),
                                     [tk], [dkey])])
                    fronts.append(front)
                    backs.append(back)
        return fronts, backs

    prep_fronts, prep_backs = [], []
    pf_cnt, pb_cnt = [0], [0]

    def enqueue_prep(e, ih, direct=False):
        fr, bk = prep_steps(e, ih, direct=direct)
        prep_fronts.extend(fr)
        prep_backs.extend(bk)

    def pump(n=10**9, banks=(4, 5)):
        PREP_PB[0] = banks
        while prep_backs and n > 0:
            while prep_fronts and pf_cnt[0] < pb_cnt[0] + 2:
                prep_fronts.pop(0)()
                pf_cnt[0] += 1
            prep_backs.pop(0)()
            pb_cnt[0] += 1
            n -= 1

    def ag_col(hc, ic):
        return ((hc // 4) * 8 + ic) * 512 + (hc % 4) * 128

    def load_AB(e, ih):
        k = e * 2 + ih
        pr.dma_group("sync", "sp", [
            (lambda s: s.dma_start(out=A_g[:, :], in_=agT_sc[k]),
             [f"agT{k}"], ["A_g"]),
        ])
        pr.dma_group("scalar", "aq", [
            (lambda s: s.dma_start(out=A_u[:, :], in_=auT_sc[k]),
             [f"auT{k}"], ["A_u"]),
        ])
        pr.dma_group("scalar", "aq", [
            (lambda s: s.dma_start(out=B_d[:, :], in_=bdT_sc[k]),
             [f"bdT{k}"], ["B_d"]),
        ])

    def ln_stats(lt, dump_lnf=False):
        Xk = f"X{lt}"
        Xsl = X[:, lt * H:(lt + 1) * H]
        ST0, ST1, ST2, ST3 = STS[lt % 8]
        s0, s1, s2, s3 = (f"ST{j}_{lt % 8}" for j in range(4))
        lt8 = lt % 8
        xlt3 = XLT[:, :].rearrange("p (hc t) -> p hc t", hc=HC)
        if dump_lnf:
            dump, dkey = LNF[:, :], "LNF"
        else:
            dump, dkey = xlt3[:, :, lt8 * 128:(lt8 + 1) * 128], f"XLT{lt8}"
        pr.op("vector", lambda v: v.reduce_sum(ST0, Xsl, axis=AX.X),
              reads=[Xk], writes=[s0], sem="dve", inc=1)
        pr.op("vector", lambda v: v.tensor_scalar(
            out=ST0, in0=ST0, scalar1=-1.0 / H, scalar2=None,
            op0=ALU.mult), reads=[s0], writes=[s0], sem="dve", inc=1)
        pr.op("scalar", lambda sc: sc.activation(
            out=dump, in_=Xsl, func=AF.Square, bias=ST0,
            accum_out=ST1),
            reads=[Xk, s0], writes=[dkey, s1], sem="act", inc=1)
        pr.op("scalar", lambda sc: sc.activation(
            out=ST2, in_=ST1, func=AF.Sqrt, scale=1.0 / H,
            bias=EPSC[:, :]), reads=[s1, "EPSC"], writes=[s2],
            sem="act", inc=1)
        pr.op("vector", lambda v: v.reciprocal(ST2, ST2),
              reads=[s2], writes=[s2], sem="dve", inc=1)
        pr.op("vector", lambda v: v.tensor_scalar(
            out=ST3, in0=ST0, scalar1=ST2, scalar2=None,
            op0=ALU.mult), reads=[s0, s2], writes=[s3], sem="dve", inc=1)

    def ln_apply(lt, with_ln2=False):
        Xk = f"X{lt}"
        Xsl = X[:, lt * H:(lt + 1) * H]
        ST0, ST1, ST2, ST3 = STS[lt % 8]
        s2, s3 = f"ST2_{lt % 8}", f"ST3_{lt % 8}"
        lt8 = lt % 8
        xlt3 = XLT[:, :].rearrange("p (hc t) -> p hc t", hc=HC)
        pr.op("scalar", lambda sc: sc.activation(
            out=LNF[:, :], in_=Xsl, func=AF.Identity, scale=ST2,
            bias=ST3),
            reads=[Xk, s2, s3], writes=["LNF"], sem="act", inc=1)
        bank = P[3] if lt % 2 == 0 else P[2]
        bkey = "P3" if lt % 2 == 0 else "P2"
        pb = bank[:, :].bitcast(BF16)
        def tr(t, pb=pb, lnf=LNF):
            for hc in range(HC):
                ins = t.transpose(pb[:, hc * 128:hc * 128 + 128],
                                  lnf[:, hc * 128:hc * 128 + 128], IDB[:, :])
            return ins
        pr.op("tensor", tr, reads=["LNF", "IDB", bkey],
              writes=[bkey], sem="pe", inc=1)
        if not with_ln2:
            pr.op("vector", lambda v, lt8=lt8, pb=pb: v.tensor_copy(
                out=xlt3[:, :, lt8 * 128:(lt8 + 1) * 128],
                in_=pb[:, :].rearrange("p (hc t) -> p hc t", hc=HC)),
                reads=[bkey], writes=[f"XLT{lt8}"], sem="dve", inc=1)
        else:
            par = lt % 2
            l2b = LN2B if par == 0 else LN2B2
            l2t = LN2T if par == 0 else LN2T2
            bk, tk2 = f"LN2B{par}", f"LN2T{par}"
            pr.op("scalar", lambda sc, l2b=l2b: sc.copy(out=l2b[:, :], in_=LNF[:, :]),
                  reads=["LNF"], writes=[bk], sem="act", inc=1)
            pr.op("vector", lambda v, l2t=l2t, pb=pb: v.tensor_copy(
                out=l2t[:, :].rearrange("p (hc t) -> p hc t", hc=HC),
                in_=pb[:, :].rearrange("p (hc t) -> p hc t", hc=HC)),
                reads=[bkey], writes=[tk2], sem="dve", inc=1)

    def ln_tile(lt, with_ln2=False):
        ln_stats(lt)
        ln_apply(lt, with_ln2)

    def routing(q):
        for lt8 in range(8):
            lt = q * 8 + lt8
            rp = P[5] if lt8 % 2 == 0 else P[6]
            rpk = "P5" if lt8 % 2 == 0 else "P6"
            def mm(t, lt8=lt8, rp=rp):
                for hc in range(HC):
                    ins = t.matmul(rp[:, 0:n_routed],
                                   XLT[:, hc * 1024 + lt8 * 128:hc * 1024 + lt8 * 128 + 128],
                                   WRT[:, hc * n_routed:(hc + 1) * n_routed],
                                   start=(hc == 0), stop=(hc == HC - 1))
                return ins
            pr.op("tensor", mm,
                  reads=["WRT", rpk, f"XLT{lt8}"], writes=[rpk],
                  sem="pe", inc=1)
            pr.op("vector", lambda v, rp=rp: v.tensor_tensor(
                out=PR0[:, :], in0=rp[:, 0:n_routed], in1=RBB[:, :], op=ALU.add),
                reads=[rpk, "RBB"], writes=["PR0"], sem="dve", inc=1)
            pr.op("scalar", lambda sc: sc.activation(
                out=PR0[:, :], in_=PR0[:, :], func=AF.Sigmoid),
                reads=["PR0"], writes=["PR0"], sem="act", inc=1)
            pr.op("vector", lambda v: v.reduce_max(M1[:, :], PR0[:, :], axis=AX.X),
                  reads=["PR0"], writes=["M1"], sem="dve", inc=1)
            pr.op("vector", lambda v: v.tensor_scalar(
                out=PR1[:, :], in0=PR0[:, :], scalar1=M1[:, :], scalar2=None,
                op0=ALU.is_ge), reads=["PR0", "M1"], writes=["PR1"],
                sem="dve", inc=1)
            pr.op("vector", lambda v: v.scalar_tensor_tensor(
                out=PR1[:, :], in0=PR1[:, :], scalar=-1e30, in1=PR0[:, :],
                op0=ALU.mult, op1=ALU.add),
                reads=["PR1", "PR0"], writes=["PR1"], sem="dve", inc=1)
            pr.op("vector", lambda v: v.reduce_max(M2[:, :], PR1[:, :], axis=AX.X),
                  reads=["PR1"], writes=["M2"], sem="dve", inc=1)
            pr.op("vector", lambda v: v.scalar_tensor_tensor(
                out=PR1[:, :], in0=PR0[:, :], scalar=M2[:, :], in1=PR0[:, :],
                op0=ALU.is_ge, op1=ALU.mult),
                reads=["PR0", "M2"], writes=["PR1"], sem="dve", inc=1)
            pr.op("vector", lambda v, lt=lt: v.tensor_scalar(
                out=CW[:, lt * n_routed:(lt + 1) * n_routed], in0=PR1[:, :],
                scalar1=0.5, scalar2=None, op0=ALU.mult),
                reads=["PR1"], writes=["CW"], sem="dve", inc=1)

    def load_A(e, ih):
        if (e, ih) in wb_done:
            k = e * 2 + ih
            pr.dma_group("sync", "sp", [
                (lambda s, k=k: s.dma_start(out=A_g[:, :], in_=agT_sc[k]),
                 [f"agT{k}"], ["A_g"]),
                (lambda s, k=k: s.dma_start(out=A_u[:, :], in_=auT_sc[k]),
                 [f"auT{k}"], ["A_u"]),
            ])
            return
        cast_expert(e)
        for la in (1, 2):
            if e + la <= n_routed:
                cast_expert(e + la)
        src_g = wgs_bf if e == 0 else wg_bf[e - 1]
        src_u = wus_bf if e == 0 else wu_bf[e - 1]
        items = []
        for hc in range(HC):
            items.append((lambda s, hc=hc, src=src_g, ih=ih: s.dma_start(
                out=A_g[:, hc * 1024:(hc + 1) * 1024],
                in_=src[ih * 1024:(ih + 1) * 1024, hc * 128:(hc + 1) * 128],
                transpose=True), [f"wbf_g{e}"], ["A_g"]))
        for hc in range(HC):
            items.append((lambda s, hc=hc, src=src_u, ih=ih: s.dma_start(
                out=A_u[:, hc * 1024:(hc + 1) * 1024],
                in_=src[ih * 1024:(ih + 1) * 1024, hc * 128:(hc + 1) * 128],
                transpose=True), [f"wbf_u{e}"], ["A_u"]))
        pr.dma_group("sync", "sp", items)
        k = e * 2 + ih
        pr.dma_group("scalar", "aq", [
            (lambda s, k=k: s.dma_start(out=agT_sc[k], in_=A_g[:, :]),
             ["A_g"], [f"agT{k}"]),
            (lambda s, k=k: s.dma_start(out=auT_sc[k], in_=A_u[:, :]),
             ["A_u"], [f"auT{k}"]),
        ])

    def load_B(e, ih):
        k = e * 2 + ih
        if (e, ih) in wb_done:
            pr.dma_group("sync", "sp", [
                (lambda s, k=k: s.dma_start(out=B_d[:, :], in_=bdT_sc[k]),
                 [f"bdT{k}"], ["B_d"])])
            return
        wb_done.add((e, ih))
        src_d = wds_bf if e == 0 else wd_bf[e - 1]
        items = []
        for ic in range(IC_H):
            col = (ih * IC_H + ic) * 128
            items.append((lambda s, ic=ic, col=col, src=src_d: s.dma_start(
                out=B_d[:, ic * 1024:(ic + 1) * 1024],
                in_=src[:, col:col + 128], transpose=True),
                [f"wbf_d{e}"], ["B_d"]))
        pr.dma_group("sync", "sp", items)
        pr.dma_group("scalar", "aq", [
            (lambda s, k=k: s.dma_start(out=bdT_sc[k], in_=B_d[:, :]),
             ["B_d"], [f"bdT{k}"])])

    def ffn_quarter_expert(q, e, ih, ln_queue=None):
        for ls2 in range(2):
            for ic in range(IC_H):
                pg = P[ic % 2]
                pgk = f"P{ic % 2}"
                gs = GS0
                gsk = "GS0"
                def gmm(t, pg=pg, ic=ic, ls2=ls2):
                    for hc in range(HC):
                        ins = t.matmul(
                            pg[:, :],
                            A_g[:, ag_col(hc, ic):ag_col(hc, ic) + 128],
                            XLT[:, hc * 1024 + ls2 * 512:hc * 1024 + ls2 * 512 + 512],
                            start=(hc == 0), stop=(hc == HC - 1))
                    return ins
                pr.op("tensor", gmm,
                      reads=["A_g", pgk] + [f"XLT{j}" for j in range(8)],
                      writes=[pgk], sem="pe", inc=1)
                pr.op("scalar", lambda sc, pg=pg, gs=gs: sc.activation(
                    out=gs[:, :], in_=pg[:, :], func=AF.Silu),
                    reads=[pgk], writes=[gsk], sem="act", inc=1)
                def umm(t, ic=ic, ls2=ls2):
                    for hc in range(HC):
                        ins = t.matmul(
                            P[2][:, :],
                            A_u[:, ag_col(hc, ic):ag_col(hc, ic) + 128],
                            XLT[:, hc * 1024 + ls2 * 512:hc * 1024 + ls2 * 512 + 512],
                            start=(hc == 0), stop=(hc == HC - 1))
                    return ins
                pr.op("tensor", umm,
                      reads=["A_u", "P2"] + [f"XLT{j}" for j in range(8)],
                      writes=["P2"], sem="pe", inc=1)
                pr.op("vector", lambda v, gs=gs, ic=ic, ls2=ls2:
                      v.tensor_tensor(
                          out=C_h[:, ic * 1024 + ls2 * 512:ic * 1024 + ls2 * 512 + 512],
                          in0=P[2][:, :], in1=gs[:, :], op=ALU.mult),
                      reads=["P2", gsk], writes=["C_h"], sem="dve", inc=1)
                pump(3, banks=(4, 5))
        for sub in range(4):
            for lsub in range(2):
                pidx = sub * 2 + lsub
                base = 4 if pidx % 2 == 0 else 6
                def dmm(t, sub=sub, lsub=lsub, base=base):
                    for ic in range(IC_H):
                        for hoc in range(2):
                            ins = t.matmul(
                                P[base + hoc][:, :],
                                C_h[:, ic * 1024 + sub * 256 + lsub * 128:
                                    ic * 1024 + sub * 256 + lsub * 128 + 128],
                                B_d[:, ic * 1024 + hoc * 512:ic * 1024 + hoc * 512 + 512],
                                start=(ic == 0), stop=(ic == IC_H - 1),
                                skip_group_check=True)
                    return ins
                pkeys = [f"P{base}", f"P{base + 1}"]
                pr.op("tensor", dmm, reads=["C_h", "B_d"] + pkeys, writes=pkeys,
                      sem="pe", inc=1)
                lt = q * 8 + sub * 2 + lsub
                for hoc in range(2):
                    xsl = X[:, lt * H + hoc * 512:lt * H + hoc * 512 + 512]
                    if e == 0:
                        sca = 0.5
                        rd = []
                    else:
                        sca = CW[:, lt * n_routed + (e - 1):lt * n_routed + e]
                        rd = ["CW"]
                    pr.op("vector", lambda v, p=P[base + hoc], xsl=xsl, sca=sca:
                          v.scalar_tensor_tensor(out=xsl, in0=p[:, :], scalar=sca,
                                                 in1=xsl, op0=ALU.mult, op1=ALU.add),
                          reads=[f"P{base + hoc}", f"X{lt}"] + rd,
                          writes=[f"X{lt}"], sem="dve", inc=1)
                pump(3, banks=(0, 1))
                if ln_queue:
                    ln_queue.pop(0)()

    prepped = set()
    first_load = [True]

    def ffn_call(skip_ln_q0=False, store_out=False, tail_stats=None):
        phases = [(e, ih) for e in range(n_routed + 1) for ih in range(2)]
        for q in range(n_q):
            if q == 0 and not skip_ln_q0:
                with _Sect("ln"):
                    for lt8 in range(8):
                        ln_stats(lt8)
                    for lt8 in range(8):
                        ln_apply(lt8)
            with _Sect("routing"):
                routing(q)
            ln_sched = []
            for idx, (e, ih) in enumerate(phases):
                if idx == 0:
                    if first_load[0]:
                        first_load[0] = False
                        prepped.add((0, 0))
                        enqueue_prep(0, 0, direct=True)
                        pump()
                        prepped.add(phases[1])
                        enqueue_prep(*phases[1])
                    elif q == 0:
                        load_AB(e, ih)
                if idx + 2 < len(phases):
                    ne2 = phases[idx + 2]
                    if ne2 not in prepped:
                        prepped.add(ne2)
                        enqueue_prep(*ne2)
                if idx == len(phases) - 2:
                    if q + 1 < n_q:
                        ln_sched += [
                            (lambda lt=(q + 1) * 8 + j: ln_stats(lt, dump_lnf=True))
                            for j in range(8)]
                        ln_sched += [
                            (lambda lt=(q + 1) * 8 + j: ln_apply(lt))
                            for j in range(8)]
                    elif tail_stats:
                        ln_sched += [
                            (lambda j=j: ln_stats(j, dump_lnf=True))
                            for j in range(8)]
                with _Sect("expert"):
                    ffn_quarter_expert(q, e, ih, ln_queue=ln_sched)
                pump(banks=(0, 1))
                if idx + 1 < len(phases):
                    load_AB(*phases[idx + 1])
                elif q + 1 < n_q:
                    load_AB(*phases[0])
            if store_out:
                pr.dma_group("sync", "sp", [
                    (lambda s, q=q: s.dma_start(
                        out=x_out3[:, q * 8:(q + 1) * 8, :],
                        in_=X3[:, q * 8:(q + 1) * 8, :]),
                     [f"X{lt}" for lt in range(q * 8, q * 8 + 8)],
                     ["x_out"])])

    def attention(stats_prefilled=False, post_ln_q0=False):
        if attn_debug:
            pr.dma_group("sync", "sp", [(lambda s: s.dma_start(
                out=d_x1.rearrange("(t p) d -> p t d", p=128),
                in_=X[:, :].rearrange("p (t d) -> p t d", t=n_lt)),
                [f"X{lt}" for lt in range(n_lt)], ["d_x1"])])
        pr.alias("QT", ["C_h"])
        pr.alias("KVS", ["A_g"])
        pr.alias("KVB", ["A_g"])
        for k in ("QPF", "KPF", "KPB", "LN2B", "LN2T", "ZS",
                  "QPF2", "KPF2", "KPB2", "LN2B2", "LN2T2"):
            pr.alias(k, ["A_u"])
        pr.alias("QPJ", ["B_d"])
        pr.alias("KPJ", ["B_d"])
        cast_small(qproj.rearrange("(hc p) e -> p hc e", p=128),
                   QPJ[:, :].rearrange("p (hc e) -> p hc e", hc=HC),
                   "QPJ", HC, PROJ)
        cast_small(kproj.rearrange("(hc p) e -> p hc e", p=128),
                   KPJ[:, :].rearrange("p (hc e) -> p hc e", hc=HC),
                   "KPJ", HC, PROJ)
        for lt in range(n_lt):
            par = lt % 2
            l2t, l2b = (LN2T, LN2B) if par == 0 else (LN2T2, LN2B2)
            qpf, kpf, kpb = (QPF, KPF, KPB) if par == 0 else (QPF2, KPF2, KPB2)
            tkq, tkk, tkb = f"QPF{par}", f"KPF{par}", f"KPB{par}"
            tk2, bk2 = f"LN2T{par}", f"LN2B{par}"
            gsl = GS0[:, par * 256:par * 256 + 256]
            gsk = f"GS0{par}"
            pq, pk = (P[0], P[1]) if par == 0 else (P[2], P[3])
            pqk, pkk = (("P0", "P1") if par == 0 else ("P2", "P3"))
            if lt == 0 and not stats_prefilled:
                for j in range(8):
                    ln_stats(j)
            ln_apply(lt, with_ln2=True)
            def qmm(t, l2t=l2t, pq=pq):
                for hc in range(HC):
                    ins = t.matmul(pq[:, 0:PROJ],
                                   l2t[:, hc * 128:(hc + 1) * 128],
                                   QPJ[:, hc * PROJ:(hc + 1) * PROJ],
                                   start=(hc == 0), stop=(hc == HC - 1),
                                   skip_group_check=True)
                return ins
            pr.op("tensor", qmm, reads=[tk2, "QPJ", pqk], writes=[pqk],
                  sem="pe", inc=1)
            pr.op("scalar", lambda sc, lt=lt, qpf=qpf, pq=pq: sc.activation(
                out=qpf[:, :], in_=pq[:, 0:PROJ], func=AF.Exp,
                accum_out=ZS[:, lt:lt + 1]),
                reads=[pqk], writes=[tkq, "ZS"], sem="act", inc=1)
            def kmm(t, l2t=l2t, pk=pk):
                for hc in range(HC):
                    ins = t.matmul(pk[:, 0:PROJ],
                                   l2t[:, hc * 128:(hc + 1) * 128],
                                   KPJ[:, hc * PROJ:(hc + 1) * PROJ],
                                   start=(hc == 0), stop=(hc == HC - 1),
                                   skip_group_check=True)
                return ins
            pr.op("tensor", kmm, reads=[tk2, "KPJ", pkk], writes=[pkk],
                  sem="pe", inc=1)
            pr.op("scalar", lambda sc, kpf=kpf, pk=pk: sc.activation(
                out=kpf[:, :], in_=pk[:, 0:PROJ], func=AF.Exp),
                reads=[pkk], writes=[tkk], sem="act", inc=1)
            pr.op("vector", lambda v, kpb=kpb, kpf=kpf: v.tensor_copy(
                out=kpb[:, :], in_=kpf[:, :]),
                reads=[tkk], writes=[tkb], sem="dve", inc=1)
            pr.op("vector", lambda v, gsl=gsl, qpf=qpf: v.tensor_copy(
                out=gsl, in_=qpf[:, :]),
                reads=[tkq], writes=[gsk], sem="dve", inc=1)
            p2b = (P[1] if par == 0 else P[0])[:, :].bitcast(BF16)
            trk = "P1" if par == 0 else "P0"
            def qtr(t, p2b=p2b, gsl=gsl):
                for ec in range(2):
                    ins = t.transpose(p2b[:, ec * 128:ec * 128 + 128],
                                      gsl[:, ec * 128:ec * 128 + 128], IDB[:, :])
                return ins
            pr.op("tensor", qtr, reads=[gsk, "IDB", trk], writes=[trk],
                  sem="pe", inc=1)
            pr.op("vector", lambda v, lt=lt, p2b=p2b: v.tensor_copy(
                out=QT[:, :].rearrange("p (ec t) -> p ec t", ec=2)
                    [:, :, lt * 128:(lt + 1) * 128],
                in_=p2b[:, 0:256].rearrange("p (ec t) -> p ec t", ec=2)),
                reads=[trk], writes=["QT"], sem="dve", inc=1)
            def kvmm(t, kpb=kpb, l2b=l2b):
                for ec in range(2):
                    for dsl in range(2):
                        ins = t.matmul(
                            P[4 + ec * 2 + dsl][:, :],
                            kpb[:, ec * 128:(ec + 1) * 128],
                            l2b[:, dsl * 512:(dsl + 1) * 512],
                            start=True, stop=True,
                            skip_group_check=True)
                return ins
            pr.op("tensor", kvmm,
                  reads=[tkb, bk2, "P4", "P5", "P6", "P7"],
                  writes=["P4", "P5", "P6", "P7"], sem="pe", inc=1)
            for ec in range(2):
                for dsl in range(2):
                    ksl = KVS[:, ec * 1024 + dsl * 512:ec * 1024 + dsl * 512 + 512]
                    if lt == 0:
                        pr.op("vector", lambda v, ec=ec, dsl=dsl, ksl=ksl:
                              v.tensor_copy(out=ksl, in_=P[4 + ec * 2 + dsl][:, :]),
                              reads=[f"P{4 + ec * 2 + dsl}"], writes=["KVS"],
                              sem="dve", inc=1)
                    else:
                        pr.op("vector", lambda v, ec=ec, dsl=dsl, ksl=ksl:
                              v.tensor_tensor(out=ksl, in0=P[4 + ec * 2 + dsl][:, :],
                                              in1=ksl, op=ALU.add),
                              reads=[f"P{4 + ec * 2 + dsl}", "KVS"], writes=["KVS"],
                              sem="dve", inc=1)
            if lt + 8 < n_lt:
                ln_stats(lt + 8)
        pr.op("vector", lambda v: v.tensor_scalar(
            out=ZI[:, :], in0=ZS[:, :], scalar1=1e-8, scalar2=None, op0=ALU.add),
            reads=["ZS"], writes=["ZI"], sem="dve", inc=1)
        pr.op("vector", lambda v: v.reciprocal(ZI[:, :], ZI[:, :]),
              reads=["ZI"], writes=["ZI"], sem="dve", inc=1)
        if attn_debug:
            pr.dma_group("sync", "sp", [(lambda s: s.dma_start(
                out=d_kvpart, in_=KVS[:, :]), ["KVS"], ["d_kvpart"])])
        pr.dma_group("sync", "sp", [(lambda s: s.dma_start(
            out=cc_in.rearrange("(ec p) d -> p ec d", p=128),
            in_=KVS[:, :].rearrange("p (ec d) -> p ec d", ec=2)),
            ["KVS"], ["cc_in"])])
        pr.op("gpsimd", lambda g: g.collective_compute(
            "AllReduce", ALU.add,
            replica_groups=[[0, 1, 2, 3], [4, 5, 6, 7]],
            ins=[cc_in], outs=[cc_out]),
            reads=["cc_in"], writes=["cc_out"], sem="cc", inc=1)
        pr.dma_group("sync", "sp", [(lambda s: s.dma_start(
            out=KVS[:, :].rearrange("p (ec d) -> p ec d", ec=2),
            in_=cc_out.rearrange("(ec p) d -> p ec d", p=128)),
            ["cc_out"], ["KVS"])])
        pr.op("vector", lambda v: v.tensor_copy(out=KVB[:, :], in_=KVS[:, :]),
              reads=["KVS"], writes=["KVB"], sem="dve", inc=1)
        if attn_debug:
            pr.dma_group("sync", "sp", [
                (lambda s: s.dma_start(out=d_kv, in_=KVS[:, :]), ["KVS"], ["d_kv"]),
                (lambda s: s.dma_start(out=d_qt, in_=QT[:, :]), ["QT"], ["d_qt"]),
                (lambda s: s.dma_start(out=d_z, in_=ZS[:, :]), ["ZS"], ["d_z"]),
            ])
        for lt in range(n_lt):
            for dsl in range(2):
                pb = (lt % 2) * 2 + dsl
                def nmm(t, lt=lt, dsl=dsl, pb=pb):
                    for ec in range(2):
                        ins = t.matmul(
                            P[pb][:, :],
                            QT[:, ec * NT + lt * 128:ec * NT + lt * 128 + 128],
                            KVB[:, ec * 1024 + dsl * 512:ec * 1024 + dsl * 512 + 512],
                            start=(ec == 0), stop=(ec == 1),
                            skip_group_check=True)
                    return ins
                pr.op("tensor", nmm, reads=["QT", "KVB", f"P{pb}"],
                      writes=[f"P{pb}"], sem="pe", inc=1)
                xsl = X[:, lt * H + dsl * 512:lt * H + dsl * 512 + 512]
                pr.op("vector", lambda v, p=P[pb], xsl=xsl, lt=lt:
                      v.scalar_tensor_tensor(
                          out=xsl, in0=p[:, :], scalar=ZI[:, lt:lt + 1],
                          in1=xsl, op0=ALU.mult, op1=ALU.add),
                      reads=[f"P{pb}", f"X{lt}", "ZI"], writes=[f"X{lt}"],
                      sem="dve", inc=1)
            if post_ln_q0:
                if lt < 8:
                    ln_stats(lt)
                if 3 <= lt < 11:
                    ln_apply(lt - 3)

    n_st = len(stages)
    for i, st in enumerate(stages):
        if st == "f":
            pr.alias("C_h", ["QT"])
            pr.alias("A_g", ["KVS", "KVB"])
            pr.alias("A_u", ["QPF", "KPF", "KPB", "LN2B", "LN2T", "ZS",
                     "QPF2", "KPF2", "KPB2", "LN2B2", "LN2T2"])
            pr.alias("B_d", ["QPJ", "KPJ"])
            ffn_call(skip_ln_q0=(i > 0 and stages[i - 1] == "a"),
                     store_out=(i == n_st - 1),
                     tail_stats=(i + 1 < n_st and stages[i + 1] == "a"))
        elif st == "a":
            attention(stats_prefilled=(i > 0 and stages[i - 1] == "f"),
                      post_ln_q0=(i + 1 < n_st and stages[i + 1] == "f"))

    if not (stages and stages[-1] == "f"):
        pr.dma_group("sync", "sp", [(lambda s: s.dma_start(
            out=x_out3, in_=X[:, :].rearrange("p (t d) -> p t d", t=n_lt)),
            [f"X{lt}" for lt in range(n_lt)], ["x_out"])])
    pr.waitpoint("sync", ["x_out"])

    sem_handles = {}
    ctxs = []
    for s in pr.sem_names:
        cm = nc.semaphore(s)
        sem_handles[s] = cm.__enter__()
        ctxs.append(cm)
    blk = nc.Block()
    block = blk.__enter__()
    pr.replay(block, sem_handles)
    blk.__exit__(None, None, None)
    for cm in reversed(ctxs):
        cm.__exit__(None, None, None)
    return nc


def kernel(L0_attn_x, L0_attn_q_proj, L0_attn_k_proj, L0_ffn_Wg_s, L0_ffn_Wu_s,
           L0_ffn_Wd_s, L0_ffn_Wg, L0_ffn_Wu, L0_ffn_Wd, L0_ffn_Wr, L0_ffn_rb):
    x = np.ascontiguousarray(np.asarray(L0_attn_x))
    B, S, _ = x.shape
    shards = x.reshape(B * 4, S // 4, H)
    wr_t = np.ascontiguousarray(np.asarray(L0_ffn_Wr).T)
    rb_b = np.ascontiguousarray(
        np.broadcast_to(np.asarray(L0_ffn_rb), (128, E_R)))
    nc = build_nc()
    common = {
        "wg_s": np.ascontiguousarray(L0_ffn_Wg_s),
        "wu_s": np.ascontiguousarray(L0_ffn_Wu_s),
        "wd_s": np.ascontiguousarray(L0_ffn_Wd_s),
        "wg_r": np.ascontiguousarray(L0_ffn_Wg),
        "wu_r": np.ascontiguousarray(L0_ffn_Wu),
        "wd_r": np.ascontiguousarray(L0_ffn_Wd),
        "wr_t": wr_t,
        "rb_b": rb_b,
        "qproj": np.ascontiguousarray(L0_attn_q_proj),
        "kproj": np.ascontiguousarray(L0_attn_k_proj),
    }
    in_maps = [dict(common, x_in=np.ascontiguousarray(shards[c]))
               for c in range(N_CORES)]
    res = run_bass_kernel_spmd(nc, in_maps, core_ids=list(range(N_CORES)))
    out = np.stack([np.asarray(res.results[c]["x_out"]) for c in range(N_CORES)])
    return out.reshape(B, S, H).astype(np.float32)


# version 14 (final)
# speedup vs baseline: 1.1141x; 1.0241x over previous
import numpy as np
import ml_dtypes
import concourse.bass as bass
import concourse.mybir as mybir
from concourse.bass_utils import run_bass_kernel_spmd

F32 = mybir.dt.float32
BF16 = mybir.dt.bfloat16
AF = mybir.ActivationFunctionType
ALU = mybir.AluOpType
AX = mybir.AxisListType

H = 1024
I = 2048
E_R = 8
PROJ = 256
N_LT = 32
N_Q = N_LT // 8
N_CORES = 8
HC = H // 128
IC_H = 8
LN_EPS = 1e-5

ENGINES = ("tensor", "vector", "scalar", "sync", "gpsimd")


PARANOID_ALL = [False]


class Prog:

    N_DMA_POOL = 6

    def __init__(self, nc):
        self.nc = nc
        self.ops = {e: [] for e in ENGINES}
        self.sem_names = ["pe", "dve", "act", "cc"]
        self.dma_pools = {}
        for q in ("sp", "aq", "pq"):
            pool = [f"{q}{i}" for i in range(self.N_DMA_POOL)]
            self.dma_pools[q] = pool
            self.sem_names += pool
        self.dma_rr = {q: 0 for q in self.dma_pools}
        self.counts = {s: 0 for s in self.sem_names}
        self.bufs = {}
        self.last_wait = {}
        self.eng_sem = {"tensor": "pe", "vector": "dve", "scalar": "act",
                        "sync": None, "gpsimd": None}

    def _dep(self, key):
        return self.bufs.setdefault(key, {"w": {}, "r": {}})

    def alias(self, dst, srcs):
        d = self._dep(dst)
        for sk in srcs:
            e = self._dep(sk)
            for s, v in e["w"].items():
                d["w"][s] = max(d["w"].get(s, 0), v)
            for s, v in e["r"].items():
                d["r"][s] = max(d["r"].get(s, 0), v)

    def op(self, eng, fn, reads=(), writes=(), sem=None, inc=0,
           extra_waits=(), no_self_sync=False):
        waits = {}
        for k in reads:
            for s, v in self._dep(k)["w"].items():
                waits[s] = max(waits.get(s, 0), v)
        for k in writes:
            d = self._dep(k)
            for s, v in d["w"].items():
                waits[s] = max(waits.get(s, 0), v)
            for s, v in d["r"].items():
                waits[s] = max(waits.get(s, 0), v)
        for s, v in extra_waits:
            waits[s] = max(waits.get(s, 0), v)
        own = self.eng_sem[eng]
        if (no_self_sync or eng == "tensor") and own in waits:
            del waits[own]
        emit = []
        for s, v in sorted(waits.items()):
            lw = self.last_wait.get((eng, s), 0)
            if v > lw:
                emit.append((s, v))
                self.last_wait[(eng, s)] = v
        if inc:
            assert sem is not None
            self.counts[sem] += inc
            val = self.counts[sem]
        else:
            sem, val = None, None
        self.ops[eng].append((emit, fn, sem, inc))
        if inc:
            for k in reads:
                d = self._dep(k)
                d["r"][sem] = max(d["r"].get(sem, 0), val)
            for k in writes:
                d = self._dep(k)
                d["w"] = {sem: val}
                d["r"] = {}
        else:
            assert not writes, f"write without inc: {writes}"
        if PARANOID_ALL[0] and inc:
            self.barrier()
        return val

    def dma_group(self, eng, queue, items):
        pool = self.dma_pools[queue]
        sem = pool[self.dma_rr[queue] % len(pool)]
        self.dma_rr[queue] += 1
        prior_end = self.counts[sem]
        if prior_end > self.last_wait.get((eng, sem), 0):
            self.ops[eng].append(([(sem, prior_end)], None, None, 0))
            self.last_wait[(eng, sem)] = prior_end
        group_end = prior_end + 16 * len(items)
        item_waits = []
        for fn, reads, writes in items:
            waits = {}
            for k in reads:
                for s, v in self._dep(k)["w"].items():
                    waits[s] = max(waits.get(s, 0), v)
            for k in writes:
                d = self._dep(k)
                for s, v in d["w"].items():
                    waits[s] = max(waits.get(s, 0), v)
                for s, v in d["r"].items():
                    waits[s] = max(waits.get(s, 0), v)
            item_waits.append(waits)
        for (fn, reads, writes), waits in zip(items, item_waits):
            emit = []
            for s, v in sorted(waits.items()):
                lw = self.last_wait.get((eng, s), 0)
                if v > lw:
                    emit.append((s, v))
                    self.last_wait[(eng, s)] = v
            self.counts[sem] += 16
            self.ops[eng].append((emit, fn, sem, 16))
        for fn, reads, writes in items:
            for k in reads:
                d = self._dep(k)
                d["r"][sem] = max(d["r"].get(sem, 0), group_end)
            for k in writes:
                d = self._dep(k)
                d["w"] = {sem: group_end}
                d["r"] = {}
        assert self.counts[sem] == group_end
        if PARANOID_ALL[0]:
            self.barrier()
        return group_end

    def barrier(self):
        snap = [(s, v) for s, v in self.counts.items() if v > 0]
        for eng in ENGINES:
            emit = []
            for s, v in snap:
                if v > self.last_wait.get((eng, s), 0):
                    emit.append((s, v))
                    self.last_wait[(eng, s)] = v
            if emit:
                self.ops[eng].append((emit, None, None, 0))

    def waitpoint(self, eng, reads):
        waits = {}
        for k in reads:
            for s, v in self._dep(k)["w"].items():
                waits[s] = max(waits.get(s, 0), v)
        emit = []
        for s, v in sorted(waits.items()):
            lw = self.last_wait.get((eng, s), 0)
            if v > lw:
                emit.append((s, v))
                self.last_wait[(eng, s)] = v
        if emit:
            self.ops[eng].append((emit, None, None, 0))

    def replay(self, block, sems):
        def make_body(eng_name):
            def body(e):
                for waits, fn, sem, inc in self.ops[eng_name]:
                    for s, v in waits:
                        e.wait_ge(sems[s], v)
                    if fn is not None:
                        ins = fn(e)
                        if inc:
                            ins.then_inc(sems[sem], inc)
            return body
        block.tensor(make_body("tensor"))
        block.vector(make_body("vector"))
        block.scalar(make_body("scalar"))
        block.sync(make_body("sync"))
        block.gpsimd(make_body("gpsimd"))


def build_nc(n_lt=N_LT, n_routed=E_R, stages="faf", attn_debug=False, paranoid=""):
    paranoid = set(str(paranoid).split(",")) if paranoid else set()
    PARANOID_ALL[0] = ("all" in paranoid)

    class _Sect:
        def __init__(self, name):
            self.name = name
        def __enter__(self):
            self.prev = PARANOID_ALL[0]
            if self.name in paranoid:
                PARANOID_ALL[0] = True
        def __exit__(self, *a):
            PARANOID_ALL[0] = self.prev
            return False
    nc = bass.Bass("TRN2", target_bir_lowering=False,
                   dynamic_dma_scratch_size=512)
    NT = n_lt * 128
    n_q = n_lt // 8

    x_in = nc.dram_tensor("x_in", [NT, H], F32, kind="ExternalInput").ap()
    wg_s = nc.dram_tensor("wg_s", [I, H], F32, kind="ExternalInput").ap()
    wu_s = nc.dram_tensor("wu_s", [I, H], F32, kind="ExternalInput").ap()
    wd_s = nc.dram_tensor("wd_s", [H, I], F32, kind="ExternalInput").ap()
    wg_r = nc.dram_tensor("wg_r", [n_routed, I, H], F32, kind="ExternalInput").ap()
    wu_r = nc.dram_tensor("wu_r", [n_routed, I, H], F32, kind="ExternalInput").ap()
    wd_r = nc.dram_tensor("wd_r", [n_routed, H, I], F32, kind="ExternalInput").ap()
    wr_t = nc.dram_tensor("wr_t", [H, n_routed], F32, kind="ExternalInput").ap()
    rb_b = nc.dram_tensor("rb_b", [128, n_routed], F32, kind="ExternalInput").ap()
    qproj = nc.dram_tensor("qproj", [H, PROJ], F32, kind="ExternalInput").ap()
    kproj = nc.dram_tensor("kproj", [H, PROJ], F32, kind="ExternalInput").ap()
    x_out = nc.dram_tensor("x_out", [NT, H], F32, kind="ExternalOutput").ap()

    idb_dram = nc.inline_tensor(np.eye(128, dtype=ml_dtypes.bfloat16), "idb_dram").ap()
    idf_dram = nc.inline_tensor(np.eye(128, dtype=np.float32), "idf_dram").ap()

    agT_sc = nc.dram_tensor("agT_sc", [(n_routed + 1) * 2, 128, HC * 1024], BF16).ap()
    auT_sc = nc.dram_tensor("auT_sc", [(n_routed + 1) * 2, 128, HC * 1024], BF16).ap()
    bdT_sc = nc.dram_tensor("bdT_sc", [(n_routed + 1) * 2, 128, IC_H * 1024], BF16).ap()
    cc_in = nc.dram_tensor("cc_in", [2 * 128, H], F32).ap()
    cc_out = nc.dram_tensor("cc_out", [2 * 128, H], F32).ap()
    if attn_debug:
        d_x1 = nc.dram_tensor("d_x1", [NT, H], F32, kind="ExternalOutput").ap()
        d_kv = nc.dram_tensor("d_kv", [128, 2 * H], F32, kind="ExternalOutput").ap()
        d_kvpart = nc.dram_tensor("d_kvpart", [128, 2 * H], F32, kind="ExternalOutput").ap()
        d_qt = nc.dram_tensor("d_qt", [128, 2 * NT], BF16, kind="ExternalOutput").ap()
        d_z = nc.dram_tensor("d_z", [128, n_lt], F32, kind="ExternalOutput").ap()

    off = [(nc.sbuf_base + 31) // 32 * 32]

    def sb(name, cols, dtype, at=None):
        nbytes = cols * (2 if dtype == BF16 else 4)
        if at is None:
            at = off[0]
            off[0] += (nbytes + 31) // 32 * 32
        else:
            assert at + nbytes <= off[0] or True
        return nc.alloc_sbuf_tensor_at(name, [128, cols], dtype, offset=at), at

    X, _ = sb("X", n_lt * H, F32)
    A_g, A_g_at = sb("A_g", HC * 1024, BF16)
    A_u, A_u_at = sb("A_u", HC * 1024, BF16)
    B_d, B_d_at = sb("B_d", IC_H * 1024, BF16)
    C_h, C_h_at = sb("C_h", IC_H * 1024, BF16)
    XLT, _ = sb("XLT", HC * 1024, BF16)
    LNF, _ = sb("LNF", 1024, BF16)
    GS0, _ = sb("GS0", 512, BF16)
    CW, _ = sb("CW", n_lt * n_routed, F32)
    PR0, _ = sb("PR0", n_routed, F32)
    PR1, _ = sb("PR1", n_routed, F32)
    STT, _ = sb("STT", 32, F32)
    STS = [[STT[:, i * 4 + j:i * 4 + j + 1] for j in range(4)] for i in range(8)]
    M1, _ = sb("M1", 1, F32)
    M2, _ = sb("M2", 1, F32)
    EPSC, _ = sb("EPSC", 1, F32)
    N_CS = 3
    CSF = [sb(f"CSF{i}", 512, F32)[0] for i in range(N_CS)]
    CSB = [sb(f"CSB{i}", 512, BF16)[0] for i in range(2)]
    TB = [sb(f"TB{i}", 512, BF16)[0] for i in range(2)]
    IDF, _ = sb("IDF", 128, F32)
    IDB, _ = sb("IDB", 128, BF16)
    RBB, _ = sb("RBB", n_routed, F32)
    WRT, _ = sb("WRT", HC * n_routed, BF16)
    ZS = None
    QT, _ = sb("QT", 2 * NT, BF16, at=C_h_at)
    KVS, _ = sb("KVS", 2 * H, F32, at=A_g_at)
    KVB, _ = sb("KVB", 2 * H, BF16, at=A_g_at + 8192)
    QPF, _ = sb("QPF", PROJ, F32, at=A_u_at)
    KPF, _ = sb("KPF", PROJ, F32, at=A_u_at + 1024)
    KPB, _ = sb("KPB", PROJ, BF16, at=A_u_at + 2048)
    LN2B, _ = sb("LN2B", 1024, BF16, at=A_u_at + 2560)
    LN2T, _ = sb("LN2T", HC * 128, BF16, at=A_u_at + 4608)
    ZS, _ = sb("ZS", n_lt, F32, at=A_u_at + 8192)
    ZI = ZS
    LN2B2, _ = sb("LN2B2", 1024, BF16, at=A_u_at + 8704)
    LN2T2, _ = sb("LN2T2", HC * 128, BF16, at=A_u_at + 10752)
    QPF2, _ = sb("QPF2", PROJ, F32, at=A_u_at + 12800)
    KPF2, _ = sb("KPF2", PROJ, F32, at=A_u_at + 13824)
    KPB2, _ = sb("KPB2", PROJ, BF16, at=A_u_at + 14848)
    QPJ, _ = sb("QPJ", HC * PROJ, BF16, at=B_d_at)
    KPJ, _ = sb("KPJ", HC * PROJ, BF16, at=B_d_at + 4096)

    assert off[0] <= nc.sbuf_top, f"SBUF over budget: {off[0]} > {nc.sbuf_top}"
    import sys
    print(f"[kernel] sbuf used: {off[0]} of {nc.sbuf_top}", file=sys.stderr)
    P = [nc.alloc_psum_tensor(f"P{i}", [128, 512], F32) for i in range(8)]

    pr = Prog(nc)

    pr.op("vector", lambda v: v.memset(EPSC[:, :], LN_EPS),
          writes=["EPSC"], sem="dve", inc=1)
    pr.dma_group("sync", "sp", [
        (lambda s: s.dma_start(out=IDB[:, :], in_=idb_dram), [], ["IDB"]),
        (lambda s: s.dma_start(out=IDF[:, :], in_=idf_dram), [], ["IDF"]),
        (lambda s: s.dma_start(out=RBB[:, :], in_=rb_b), [], ["RBB"]),
    ])
    x_in3 = x_in.rearrange("(t p) d -> p t d", p=128)
    x_out3 = x_out.rearrange("(t p) d -> p t d", p=128)
    X3 = X[:, :].rearrange("p (t d) -> p t d", t=n_lt)
    pr.dma_group("sync", "sp", [
        (lambda s: s.dma_start(out=X3[:, 0:4, :], in_=x_in3[:, 0:4, :]),
         [], [f"X{lt}" for lt in range(4)])])
    pr.dma_group("scalar", "aq", [
        (lambda s: s.dma_start(out=X3[:, 4:8, :], in_=x_in3[:, 4:8, :]),
         [], [f"X{lt}" for lt in range(4, 8)])])
    for q in range(1, n_q):
        pr.dma_group("scalar", "aq", [
            (lambda s, q=q: s.dma_start(
                out=X3[:, q * 8:(q + 1) * 8, :],
                in_=x_in3[:, q * 8:(q + 1) * 8, :]),
             [], [f"X{lt}" for lt in range(q * 8, q * 8 + 8)])])
    cast_ci = [0]

    def cast_small(src3d, dst3d, dstkey, nchunk, per):
        for i in range(nchunk):
            b = cast_ci[0] % 2
            cast_ci[0] += 1
            fk, bk = f"CSF{b}", f"CSB{b}"
            pr.dma_group("sync", "sp", [
                (lambda s, i=i, b=b: s.dma_start(
                    out=CSF[b][:, 0:per], in_=src3d[:, i, :]),
                 [], [fk])])
            pr.op("scalar", lambda sc, b=b: sc.copy(
                out=CSB[b][:, 0:per], in_=CSF[b][:, 0:per]),
                reads=[fk], writes=[bk], sem="act", inc=1)
            pr.op("vector", lambda v, i=i, b=b: v.tensor_copy(
                out=dst3d[:, i, :], in_=CSB[b][:, 0:per]),
                reads=[bk], writes=[dstkey], sem="dve", inc=1)

    cast_small(wr_t.rearrange("(hc p) e -> p hc e", p=128),
               WRT[:, :].rearrange("p (hc e) -> p hc e", hc=HC),
               "WRT", HC, n_routed)

    p3b = P[3][:, :].bitcast(BF16)

    PREP_PB = [(4, 5)]

    def prep_steps(e, ih, direct=False):
        k = e * 2 + ih
        srcs = [(wg_s if e == 0 else wg_r[e - 1], agT_sc, f"agT{k}", True),
                (wu_s if e == 0 else wu_r[e - 1], auT_sc, f"auT{k}", True),
                (wd_s if e == 0 else wd_r[e - 1], bdT_sc, f"bdT{k}", False)]
        fronts, backs = [], []
        for src, dstT, dkey, is_gu in srcs:
            for rb in range(8):
                for cb in range(2):
                    ci = cast_ci[0]
                    cast_ci[0] += 1
                    b = ci % N_CS
                    tb = ci % 2
                    fk, bk, tk = f"CSF{b}", f"CSB{tb}", f"TB{tb}"
                    if is_gu:
                        r0, c0 = ih * 1024 + rb * 128, cb * 512
                    else:
                        r0, c0 = rb * 128, ih * 1024 + cb * 512

                    def front(src=src, b=b, tb=tb, fk=fk, bk=bk,
                              r0=r0, c0=c0):
                        pr.dma_group("sync", "sp", [
                            (lambda s: s.dma_start(
                                out=CSF[b][:, :], in_=src[r0:r0 + 128, c0:c0 + 512]),
                             [], [fk])])
                        pr.op("scalar", lambda sc: sc.copy(
                            out=CSB[tb][:, :], in_=CSF[b][:, :]),
                            reads=[fk], writes=[bk], sem="act", inc=1)

                    def back(dstT=dstT, dkey=dkey, is_gu=is_gu, rb=rb,
                             cb=cb, k=k, tb=tb, bk=bk, tk=tk, r0=r0,
                             direct=direct):
                        pi = PREP_PB[0][pb_cnt[0] % 2]
                        pkey = f"P{pi}"
                        pib = P[pi][:, :].bitcast(BF16)
                        def tr(t, pib=pib):
                            for tt in range(4):
                                ins = t.transpose(
                                    pib[:, tt * 128:tt * 128 + 128],
                                    CSB[tb][:, tt * 128:tt * 128 + 128],
                                    IDB[:, :])
                            return ins
                        pr.op("tensor", tr, reads=[bk, "IDB", pkey],
                              writes=[pkey], sem="pe", inc=1)
                        if is_gu:
                            col0 = (cb * 8 + rb) * 512
                            if direct:
                                dsb, dsk = ((A_g, "A_g") if dkey[:3] == "agT"
                                            else (A_u, "A_u"))
                                sl = dsb[:, col0:col0 + 512]
                                pr.op("vector", lambda v, pib=pib, sl=sl:
                                      v.tensor_copy(out=sl, in_=pib[:, 0:512]),
                                      reads=[pkey], writes=[dsk],
                                      sem="dve", inc=1)
                                pr.dma_group("scalar", "aq", [
                                    (lambda s, sl=sl: s.dma_start(
                                        out=dstT[k][:, col0:col0 + 512],
                                        in_=sl), [dsk], [dkey])])
                            else:
                                pr.op("vector", lambda v, pib=pib: v.tensor_copy(
                                    out=TB[tb][:, :], in_=pib[:, 0:512]),
                                    reads=[pkey], writes=[tk], sem="dve", inc=1)
                                pr.dma_group("scalar", "aq", [
                                    (lambda s: s.dma_start(
                                        out=dstT[k][:, col0:col0 + 512],
                                        in_=TB[tb][:, :]), [tk], [dkey])])
                        else:
                            bd3 = dstT[k].rearrange("p (ic o) -> p ic o", ic=IC_H)
                            if direct:
                                Bd3 = B_d[:, :].rearrange(
                                    "p (ic o) -> p ic o", ic=IC_H)
                                sl = Bd3[:, cb * 4:cb * 4 + 4, r0:r0 + 128]
                                pr.op("vector", lambda v, pib=pib, sl=sl:
                                      v.tensor_copy(
                                          out=sl,
                                          in_=pib[:, 0:512].rearrange(
                                              "p (t o) -> p t o", t=4)),
                                      reads=[pkey], writes=["B_d"],
                                      sem="dve", inc=1)
                                pr.dma_group("scalar", "aq", [
                                    (lambda s, sl=sl: s.dma_start(
                                        out=bd3[:, cb * 4:cb * 4 + 4, r0:r0 + 128],
                                        in_=sl), ["B_d"], [dkey])])
                            else:
                                pr.op("vector", lambda v, pib=pib: v.tensor_copy(
                                    out=TB[tb][:, :], in_=pib[:, 0:512]),
                                    reads=[pkey], writes=[tk], sem="dve", inc=1)
                                pr.dma_group("scalar", "aq", [
                                    (lambda s: s.dma_start(
                                        out=bd3[:, cb * 4:cb * 4 + 4, r0:r0 + 128],
                                        in_=TB[tb][:, :].rearrange(
                                            "p (t o) -> p t o", t=4)),
                                     [tk], [dkey])])
                    fronts.append(front)
                    backs.append(back)
        return fronts, backs

    prep_fronts, prep_backs = [], []
    pf_cnt, pb_cnt = [0], [0]

    def enqueue_prep(e, ih, direct=False):
        fr, bk = prep_steps(e, ih, direct=direct)
        prep_fronts.extend(fr)
        prep_backs.extend(bk)

    def pump(n=10**9, banks=(4, 5)):
        PREP_PB[0] = banks
        while prep_backs and n > 0:
            while prep_fronts and pf_cnt[0] < pb_cnt[0] + 2:
                prep_fronts.pop(0)()
                pf_cnt[0] += 1
            prep_backs.pop(0)()
            pb_cnt[0] += 1
            n -= 1

    def ag_col(hc, ic):
        return ((hc // 4) * 8 + ic) * 512 + (hc % 4) * 128

    def load_AB(e, ih):
        k = e * 2 + ih
        pr.dma_group("sync", "sp", [
            (lambda s: s.dma_start(out=A_g[:, :], in_=agT_sc[k]),
             [f"agT{k}"], ["A_g"]),
        ])
        pr.dma_group("scalar", "aq", [
            (lambda s: s.dma_start(out=A_u[:, :], in_=auT_sc[k]),
             [f"auT{k}"], ["A_u"]),
        ])
        pr.dma_group("scalar", "aq", [
            (lambda s: s.dma_start(out=B_d[:, :], in_=bdT_sc[k]),
             [f"bdT{k}"], ["B_d"]),
        ])

    def ln_stats(lt, dump_lnf=False):
        Xk = f"X{lt}"
        Xsl = X[:, lt * H:(lt + 1) * H]
        ST0, ST1, ST2, ST3 = STS[lt % 8]
        s0, s1, s2, s3 = (f"ST{j}_{lt % 8}" for j in range(4))
        lt8 = lt % 8
        xlt3 = XLT[:, :].rearrange("p (hc t) -> p hc t", hc=HC)
        if dump_lnf:
            dump, dkey = LNF[:, :], "LNF"
        else:
            dump, dkey = xlt3[:, :, lt8 * 128:(lt8 + 1) * 128], f"XLT{lt8}"
        pr.op("vector", lambda v: v.reduce_sum(ST0, Xsl, axis=AX.X),
              reads=[Xk], writes=[s0], sem="dve", inc=1)
        pr.op("vector", lambda v: v.tensor_scalar(
            out=ST0, in0=ST0, scalar1=-1.0 / H, scalar2=None,
            op0=ALU.mult), reads=[s0], writes=[s0], sem="dve", inc=1)
        pr.op("scalar", lambda sc: sc.activation(
            out=dump, in_=Xsl, func=AF.Square, bias=ST0,
            accum_out=ST1),
            reads=[Xk, s0], writes=[dkey, s1], sem="act", inc=1)
        pr.op("scalar", lambda sc: sc.activation(
            out=ST2, in_=ST1, func=AF.Sqrt, scale=1.0 / H,
            bias=EPSC[:, :]), reads=[s1, "EPSC"], writes=[s2],
            sem="act", inc=1)
        pr.op("vector", lambda v: v.reciprocal(ST2, ST2),
              reads=[s2], writes=[s2], sem="dve", inc=1)
        pr.op("vector", lambda v: v.tensor_scalar(
            out=ST3, in0=ST0, scalar1=ST2, scalar2=None,
            op0=ALU.mult), reads=[s0, s2], writes=[s3], sem="dve", inc=1)

    def ln_apply(lt, with_ln2=False):
        Xk = f"X{lt}"
        Xsl = X[:, lt * H:(lt + 1) * H]
        ST0, ST1, ST2, ST3 = STS[lt % 8]
        s2, s3 = f"ST2_{lt % 8}", f"ST3_{lt % 8}"
        lt8 = lt % 8
        xlt3 = XLT[:, :].rearrange("p (hc t) -> p hc t", hc=HC)
        pr.op("scalar", lambda sc: sc.activation(
            out=LNF[:, :], in_=Xsl, func=AF.Identity, scale=ST2,
            bias=ST3),
            reads=[Xk, s2, s3], writes=["LNF"], sem="act", inc=1)
        bank = P[3] if lt % 2 == 0 else P[2]
        bkey = "P3" if lt % 2 == 0 else "P2"
        pb = bank[:, :].bitcast(BF16)
        def tr(t, pb=pb, lnf=LNF):
            for hc in range(HC):
                ins = t.transpose(pb[:, hc * 128:hc * 128 + 128],
                                  lnf[:, hc * 128:hc * 128 + 128], IDB[:, :])
            return ins
        pr.op("tensor", tr, reads=["LNF", "IDB", bkey],
              writes=[bkey], sem="pe", inc=1)
        if not with_ln2:
            pr.op("vector", lambda v, lt8=lt8, pb=pb: v.tensor_copy(
                out=xlt3[:, :, lt8 * 128:(lt8 + 1) * 128],
                in_=pb[:, :].rearrange("p (hc t) -> p hc t", hc=HC)),
                reads=[bkey], writes=[f"XLT{lt8}"], sem="dve", inc=1)
        else:
            par = lt % 2
            l2b = LN2B if par == 0 else LN2B2
            l2t = LN2T if par == 0 else LN2T2
            bk, tk2 = f"LN2B{par}", f"LN2T{par}"
            pr.op("scalar", lambda sc, l2b=l2b: sc.copy(out=l2b[:, :], in_=LNF[:, :]),
                  reads=["LNF"], writes=[bk], sem="act", inc=1)
            pr.op("vector", lambda v, l2t=l2t, pb=pb: v.tensor_copy(
                out=l2t[:, :].rearrange("p (hc t) -> p hc t", hc=HC),
                in_=pb[:, :].rearrange("p (hc t) -> p hc t", hc=HC)),
                reads=[bkey], writes=[tk2], sem="dve", inc=1)

    def ln_tile(lt, with_ln2=False):
        ln_stats(lt)
        ln_apply(lt, with_ln2)

    def routing(q):
        for lt8 in range(8):
            lt = q * 8 + lt8
            rp = P[5] if lt8 % 2 == 0 else P[6]
            rpk = "P5" if lt8 % 2 == 0 else "P6"
            def mm(t, lt8=lt8, rp=rp):
                for hc in range(HC):
                    ins = t.matmul(rp[:, 0:n_routed],
                                   XLT[:, hc * 1024 + lt8 * 128:hc * 1024 + lt8 * 128 + 128],
                                   WRT[:, hc * n_routed:(hc + 1) * n_routed],
                                   start=(hc == 0), stop=(hc == HC - 1))
                return ins
            pr.op("tensor", mm,
                  reads=["WRT", rpk, f"XLT{lt8}"], writes=[rpk],
                  sem="pe", inc=1)
            pr.op("vector", lambda v, rp=rp: v.tensor_tensor(
                out=PR0[:, :], in0=rp[:, 0:n_routed], in1=RBB[:, :], op=ALU.add),
                reads=[rpk, "RBB"], writes=["PR0"], sem="dve", inc=1)
            pr.op("scalar", lambda sc: sc.activation(
                out=PR0[:, :], in_=PR0[:, :], func=AF.Sigmoid),
                reads=["PR0"], writes=["PR0"], sem="act", inc=1)
            pr.op("vector", lambda v: v.reduce_max(M1[:, :], PR0[:, :], axis=AX.X),
                  reads=["PR0"], writes=["M1"], sem="dve", inc=1)
            pr.op("vector", lambda v: v.tensor_scalar(
                out=PR1[:, :], in0=PR0[:, :], scalar1=M1[:, :], scalar2=None,
                op0=ALU.is_ge), reads=["PR0", "M1"], writes=["PR1"],
                sem="dve", inc=1)
            pr.op("vector", lambda v: v.scalar_tensor_tensor(
                out=PR1[:, :], in0=PR1[:, :], scalar=-1e30, in1=PR0[:, :],
                op0=ALU.mult, op1=ALU.add),
                reads=["PR1", "PR0"], writes=["PR1"], sem="dve", inc=1)
            pr.op("vector", lambda v: v.reduce_max(M2[:, :], PR1[:, :], axis=AX.X),
                  reads=["PR1"], writes=["M2"], sem="dve", inc=1)
            pr.op("vector", lambda v: v.scalar_tensor_tensor(
                out=PR1[:, :], in0=PR0[:, :], scalar=M2[:, :], in1=PR0[:, :],
                op0=ALU.is_ge, op1=ALU.mult),
                reads=["PR0", "M2"], writes=["PR1"], sem="dve", inc=1)
            pr.op("vector", lambda v, lt=lt: v.tensor_scalar(
                out=CW[:, lt * n_routed:(lt + 1) * n_routed], in0=PR1[:, :],
                scalar1=0.5, scalar2=None, op0=ALU.mult),
                reads=["PR1"], writes=["CW"], sem="dve", inc=1)

    def load_A(e, ih):
        if (e, ih) in wb_done:
            k = e * 2 + ih
            pr.dma_group("sync", "sp", [
                (lambda s, k=k: s.dma_start(out=A_g[:, :], in_=agT_sc[k]),
                 [f"agT{k}"], ["A_g"]),
                (lambda s, k=k: s.dma_start(out=A_u[:, :], in_=auT_sc[k]),
                 [f"auT{k}"], ["A_u"]),
            ])
            return
        cast_expert(e)
        for la in (1, 2):
            if e + la <= n_routed:
                cast_expert(e + la)
        src_g = wgs_bf if e == 0 else wg_bf[e - 1]
        src_u = wus_bf if e == 0 else wu_bf[e - 1]
        items = []
        for hc in range(HC):
            items.append((lambda s, hc=hc, src=src_g, ih=ih: s.dma_start(
                out=A_g[:, hc * 1024:(hc + 1) * 1024],
                in_=src[ih * 1024:(ih + 1) * 1024, hc * 128:(hc + 1) * 128],
                transpose=True), [f"wbf_g{e}"], ["A_g"]))
        for hc in range(HC):
            items.append((lambda s, hc=hc, src=src_u, ih=ih: s.dma_start(
                out=A_u[:, hc * 1024:(hc + 1) * 1024],
                in_=src[ih * 1024:(ih + 1) * 1024, hc * 128:(hc + 1) * 128],
                transpose=True), [f"wbf_u{e}"], ["A_u"]))
        pr.dma_group("sync", "sp", items)
        k = e * 2 + ih
        pr.dma_group("scalar", "aq", [
            (lambda s, k=k: s.dma_start(out=agT_sc[k], in_=A_g[:, :]),
             ["A_g"], [f"agT{k}"]),
            (lambda s, k=k: s.dma_start(out=auT_sc[k], in_=A_u[:, :]),
             ["A_u"], [f"auT{k}"]),
        ])

    def load_B(e, ih):
        k = e * 2 + ih
        if (e, ih) in wb_done:
            pr.dma_group("sync", "sp", [
                (lambda s, k=k: s.dma_start(out=B_d[:, :], in_=bdT_sc[k]),
                 [f"bdT{k}"], ["B_d"])])
            return
        wb_done.add((e, ih))
        src_d = wds_bf if e == 0 else wd_bf[e - 1]
        items = []
        for ic in range(IC_H):
            col = (ih * IC_H + ic) * 128
            items.append((lambda s, ic=ic, col=col, src=src_d: s.dma_start(
                out=B_d[:, ic * 1024:(ic + 1) * 1024],
                in_=src[:, col:col + 128], transpose=True),
                [f"wbf_d{e}"], ["B_d"]))
        pr.dma_group("sync", "sp", items)
        pr.dma_group("scalar", "aq", [
            (lambda s, k=k: s.dma_start(out=bdT_sc[k], in_=B_d[:, :]),
             ["B_d"], [f"bdT{k}"])])

    def ffn_quarter_expert(q, e, ih, ln_queue=None):
        for ls2 in range(2):
            for ic in range(IC_H):
                pg = P[ic % 2]
                pgk = f"P{ic % 2}"
                gs = GS0
                gsk = "GS0"
                def gmm(t, pg=pg, ic=ic, ls2=ls2):
                    for hc in range(HC):
                        ins = t.matmul(
                            pg[:, :],
                            A_g[:, ag_col(hc, ic):ag_col(hc, ic) + 128],
                            XLT[:, hc * 1024 + ls2 * 512:hc * 1024 + ls2 * 512 + 512],
                            start=(hc == 0), stop=(hc == HC - 1))
                    return ins
                pr.op("tensor", gmm,
                      reads=["A_g", pgk] + [f"XLT{j}" for j in range(8)],
                      writes=[pgk], sem="pe", inc=1)
                pr.op("scalar", lambda sc, pg=pg, gs=gs: sc.activation(
                    out=gs[:, :], in_=pg[:, :], func=AF.Silu),
                    reads=[pgk], writes=[gsk], sem="act", inc=1)
                def umm(t, ic=ic, ls2=ls2):
                    for hc in range(HC):
                        ins = t.matmul(
                            P[2][:, :],
                            A_u[:, ag_col(hc, ic):ag_col(hc, ic) + 128],
                            XLT[:, hc * 1024 + ls2 * 512:hc * 1024 + ls2 * 512 + 512],
                            start=(hc == 0), stop=(hc == HC - 1))
                    return ins
                pr.op("tensor", umm,
                      reads=["A_u", "P2"] + [f"XLT{j}" for j in range(8)],
                      writes=["P2"], sem="pe", inc=1)
                pr.op("vector", lambda v, gs=gs, ic=ic, ls2=ls2:
                      v.tensor_tensor(
                          out=C_h[:, ic * 1024 + ls2 * 512:ic * 1024 + ls2 * 512 + 512],
                          in0=P[2][:, :], in1=gs[:, :], op=ALU.mult),
                      reads=["P2", gsk], writes=["C_h"], sem="dve", inc=1)
                pump(2, banks=(4, 5))
        for sub in range(4):
            for lsub in range(2):
                pidx = sub * 2 + lsub
                base = 4 if pidx % 2 == 0 else 6
                def dmm(t, sub=sub, lsub=lsub, base=base):
                    for ic in range(IC_H):
                        for hoc in range(2):
                            ins = t.matmul(
                                P[base + hoc][:, :],
                                C_h[:, ic * 1024 + sub * 256 + lsub * 128:
                                    ic * 1024 + sub * 256 + lsub * 128 + 128],
                                B_d[:, ic * 1024 + hoc * 512:ic * 1024 + hoc * 512 + 512],
                                start=(ic == 0), stop=(ic == IC_H - 1),
                                skip_group_check=True)
                    return ins
                pkeys = [f"P{base}", f"P{base + 1}"]
                pr.op("tensor", dmm, reads=["C_h", "B_d"] + pkeys, writes=pkeys,
                      sem="pe", inc=1)
                lt = q * 8 + sub * 2 + lsub
                for hoc in range(2):
                    xsl = X[:, lt * H + hoc * 512:lt * H + hoc * 512 + 512]
                    if e == 0:
                        sca = 0.5
                        rd = []
                    else:
                        sca = CW[:, lt * n_routed + (e - 1):lt * n_routed + e]
                        rd = ["CW"]
                    pr.op("vector", lambda v, p=P[base + hoc], xsl=xsl, sca=sca:
                          v.scalar_tensor_tensor(out=xsl, in0=p[:, :], scalar=sca,
                                                 in1=xsl, op0=ALU.mult, op1=ALU.add),
                          reads=[f"P{base + hoc}", f"X{lt}"] + rd,
                          writes=[f"X{lt}"], sem="dve", inc=1)
                pump(2, banks=(0, 1))
                if ln_queue:
                    ln_queue.pop(0)()

    prepped = set()
    first_load = [True]

    def ffn_call(skip_ln_q0=False, store_out=False, tail_stats=None):
        phases = [(e, ih) for e in range(n_routed + 1) for ih in range(2)]
        for q in range(n_q):
            if q == 0 and not skip_ln_q0:
                with _Sect("ln"):
                    for lt8 in range(8):
                        ln_stats(lt8)
                    for lt8 in range(8):
                        ln_apply(lt8)
            with _Sect("routing"):
                routing(q)
            ln_sched = []
            for idx, (e, ih) in enumerate(phases):
                if idx == 0:
                    if first_load[0]:
                        first_load[0] = False
                        prepped.add((0, 0))
                        enqueue_prep(0, 0, direct=True)
                        pump()
                        prepped.add(phases[1])
                        enqueue_prep(*phases[1])
                    elif q == 0:
                        load_AB(e, ih)
                if idx + 2 < len(phases):
                    ne2 = phases[idx + 2]
                    if ne2 not in prepped:
                        prepped.add(ne2)
                        enqueue_prep(*ne2)
                if idx == len(phases) - 2:
                    if q + 1 < n_q:
                        ln_sched += [
                            (lambda lt=(q + 1) * 8 + j: ln_stats(lt, dump_lnf=True))
                            for j in range(8)]
                        ln_sched += [
                            (lambda lt=(q + 1) * 8 + j: ln_apply(lt))
                            for j in range(8)]
                    elif tail_stats:
                        ln_sched += [
                            (lambda j=j: ln_stats(j, dump_lnf=True))
                            for j in range(8)]
                with _Sect("expert"):
                    ffn_quarter_expert(q, e, ih, ln_queue=ln_sched)
                pump(banks=(0, 1))
                if idx + 1 < len(phases):
                    load_AB(*phases[idx + 1])
                elif q + 1 < n_q:
                    load_AB(*phases[0])
            if store_out:
                pr.dma_group("sync", "sp", [
                    (lambda s, q=q: s.dma_start(
                        out=x_out3[:, q * 8:(q + 1) * 8, :],
                        in_=X3[:, q * 8:(q + 1) * 8, :]),
                     [f"X{lt}" for lt in range(q * 8, q * 8 + 8)],
                     ["x_out"])])

    def attention(stats_prefilled=False, post_ln_q0=False):
        if attn_debug:
            pr.dma_group("sync", "sp", [(lambda s: s.dma_start(
                out=d_x1.rearrange("(t p) d -> p t d", p=128),
                in_=X[:, :].rearrange("p (t d) -> p t d", t=n_lt)),
                [f"X{lt}" for lt in range(n_lt)], ["d_x1"])])
        pr.alias("QT", ["C_h"])
        pr.alias("KVS", ["A_g"])
        pr.alias("KVB", ["A_g"])
        for k in ("QPF", "KPF", "KPB", "LN2B", "LN2T", "ZS",
                  "QPF2", "KPF2", "KPB2", "LN2B2", "LN2T2"):
            pr.alias(k, ["A_u"])
        pr.alias("QPJ", ["B_d"])
        pr.alias("KPJ", ["B_d"])
        cast_small(qproj.rearrange("(hc p) e -> p hc e", p=128),
                   QPJ[:, :].rearrange("p (hc e) -> p hc e", hc=HC),
                   "QPJ", HC, PROJ)
        cast_small(kproj.rearrange("(hc p) e -> p hc e", p=128),
                   KPJ[:, :].rearrange("p (hc e) -> p hc e", hc=HC),
                   "KPJ", HC, PROJ)
        for lt in range(n_lt):
            par = lt % 2
            l2t, l2b = (LN2T, LN2B) if par == 0 else (LN2T2, LN2B2)
            qpf, kpf, kpb = (QPF, KPF, KPB) if par == 0 else (QPF2, KPF2, KPB2)
            tkq, tkk, tkb = f"QPF{par}", f"KPF{par}", f"KPB{par}"
            tk2, bk2 = f"LN2T{par}", f"LN2B{par}"
            gsl = GS0[:, par * 256:par * 256 + 256]
            gsk = f"GS0{par}"
            pq, pk = (P[0], P[1]) if par == 0 else (P[2], P[3])
            pqk, pkk = (("P0", "P1") if par == 0 else ("P2", "P3"))
            if lt == 0 and not stats_prefilled:
                for j in range(8):
                    ln_stats(j)
            ln_apply(lt, with_ln2=True)
            def qmm(t, l2t=l2t, pq=pq):
                for hc in range(HC):
                    ins = t.matmul(pq[:, 0:PROJ],
                                   l2t[:, hc * 128:(hc + 1) * 128],
                                   QPJ[:, hc * PROJ:(hc + 1) * PROJ],
                                   start=(hc == 0), stop=(hc == HC - 1),
                                   skip_group_check=True)
                return ins
            pr.op("tensor", qmm, reads=[tk2, "QPJ", pqk], writes=[pqk],
                  sem="pe", inc=1)
            pr.op("scalar", lambda sc, lt=lt, qpf=qpf, pq=pq: sc.activation(
                out=qpf[:, :], in_=pq[:, 0:PROJ], func=AF.Exp,
                accum_out=ZS[:, lt:lt + 1]),
                reads=[pqk], writes=[tkq, "ZS"], sem="act", inc=1)
            def kmm(t, l2t=l2t, pk=pk):
                for hc in range(HC):
                    ins = t.matmul(pk[:, 0:PROJ],
                                   l2t[:, hc * 128:(hc + 1) * 128],
                                   KPJ[:, hc * PROJ:(hc + 1) * PROJ],
                                   start=(hc == 0), stop=(hc == HC - 1),
                                   skip_group_check=True)
                return ins
            pr.op("tensor", kmm, reads=[tk2, "KPJ", pkk], writes=[pkk],
                  sem="pe", inc=1)
            pr.op("scalar", lambda sc, kpf=kpf, pk=pk: sc.activation(
                out=kpf[:, :], in_=pk[:, 0:PROJ], func=AF.Exp),
                reads=[pkk], writes=[tkk], sem="act", inc=1)
            pr.op("vector", lambda v, kpb=kpb, kpf=kpf: v.tensor_copy(
                out=kpb[:, :], in_=kpf[:, :]),
                reads=[tkk], writes=[tkb], sem="dve", inc=1)
            pr.op("vector", lambda v, gsl=gsl, qpf=qpf: v.tensor_copy(
                out=gsl, in_=qpf[:, :]),
                reads=[tkq], writes=[gsk], sem="dve", inc=1)
            p2b = (P[1] if par == 0 else P[0])[:, :].bitcast(BF16)
            trk = "P1" if par == 0 else "P0"
            def qtr(t, p2b=p2b, gsl=gsl):
                for ec in range(2):
                    ins = t.transpose(p2b[:, ec * 128:ec * 128 + 128],
                                      gsl[:, ec * 128:ec * 128 + 128], IDB[:, :])
                return ins
            pr.op("tensor", qtr, reads=[gsk, "IDB", trk], writes=[trk],
                  sem="pe", inc=1)
            pr.op("vector", lambda v, lt=lt, p2b=p2b: v.tensor_copy(
                out=QT[:, :].rearrange("p (ec t) -> p ec t", ec=2)
                    [:, :, lt * 128:(lt + 1) * 128],
                in_=p2b[:, 0:256].rearrange("p (ec t) -> p ec t", ec=2)),
                reads=[trk], writes=["QT"], sem="dve", inc=1)
            def kvmm(t, kpb=kpb, l2b=l2b):
                for ec in range(2):
                    for dsl in range(2):
                        ins = t.matmul(
                            P[4 + ec * 2 + dsl][:, :],
                            kpb[:, ec * 128:(ec + 1) * 128],
                            l2b[:, dsl * 512:(dsl + 1) * 512],
                            start=True, stop=True,
                            skip_group_check=True)
                return ins
            pr.op("tensor", kvmm,
                  reads=[tkb, bk2, "P4", "P5", "P6", "P7"],
                  writes=["P4", "P5", "P6", "P7"], sem="pe", inc=1)
            for ec in range(2):
                for dsl in range(2):
                    ksl = KVS[:, ec * 1024 + dsl * 512:ec * 1024 + dsl * 512 + 512]
                    if lt == 0:
                        pr.op("vector", lambda v, ec=ec, dsl=dsl, ksl=ksl:
                              v.tensor_copy(out=ksl, in_=P[4 + ec * 2 + dsl][:, :]),
                              reads=[f"P{4 + ec * 2 + dsl}"], writes=["KVS"],
                              sem="dve", inc=1)
                    else:
                        pr.op("vector", lambda v, ec=ec, dsl=dsl, ksl=ksl:
                              v.tensor_tensor(out=ksl, in0=P[4 + ec * 2 + dsl][:, :],
                                              in1=ksl, op=ALU.add),
                              reads=[f"P{4 + ec * 2 + dsl}", "KVS"], writes=["KVS"],
                              sem="dve", inc=1)
            if lt + 8 < n_lt:
                ln_stats(lt + 8)
        pr.op("vector", lambda v: v.tensor_scalar(
            out=ZI[:, :], in0=ZS[:, :], scalar1=1e-8, scalar2=None, op0=ALU.add),
            reads=["ZS"], writes=["ZI"], sem="dve", inc=1)
        pr.op("vector", lambda v: v.reciprocal(ZI[:, :], ZI[:, :]),
              reads=["ZI"], writes=["ZI"], sem="dve", inc=1)
        if attn_debug:
            pr.dma_group("sync", "sp", [(lambda s: s.dma_start(
                out=d_kvpart, in_=KVS[:, :]), ["KVS"], ["d_kvpart"])])
        pr.dma_group("sync", "sp", [(lambda s: s.dma_start(
            out=cc_in.rearrange("(ec p) d -> p ec d", p=128),
            in_=KVS[:, :].rearrange("p (ec d) -> p ec d", ec=2)),
            ["KVS"], ["cc_in"])])
        pr.op("gpsimd", lambda g: g.collective_compute(
            "AllReduce", ALU.add,
            replica_groups=[[0, 1, 2, 3], [4, 5, 6, 7]],
            ins=[cc_in], outs=[cc_out]),
            reads=["cc_in"], writes=["cc_out"], sem="cc", inc=1)
        pr.dma_group("sync", "sp", [(lambda s: s.dma_start(
            out=KVS[:, :].rearrange("p (ec d) -> p ec d", ec=2),
            in_=cc_out.rearrange("(ec p) d -> p ec d", p=128)),
            ["cc_out"], ["KVS"])])
        pr.op("vector", lambda v: v.tensor_copy(out=KVB[:, :], in_=KVS[:, :]),
              reads=["KVS"], writes=["KVB"], sem="dve", inc=1)
        if attn_debug:
            pr.dma_group("sync", "sp", [
                (lambda s: s.dma_start(out=d_kv, in_=KVS[:, :]), ["KVS"], ["d_kv"]),
                (lambda s: s.dma_start(out=d_qt, in_=QT[:, :]), ["QT"], ["d_qt"]),
                (lambda s: s.dma_start(out=d_z, in_=ZS[:, :]), ["ZS"], ["d_z"]),
            ])
        for lt in range(n_lt):
            for dsl in range(2):
                pb = (lt % 2) * 2 + dsl
                def nmm(t, lt=lt, dsl=dsl, pb=pb):
                    for ec in range(2):
                        ins = t.matmul(
                            P[pb][:, :],
                            QT[:, ec * NT + lt * 128:ec * NT + lt * 128 + 128],
                            KVB[:, ec * 1024 + dsl * 512:ec * 1024 + dsl * 512 + 512],
                            start=(ec == 0), stop=(ec == 1),
                            skip_group_check=True)
                    return ins
                pr.op("tensor", nmm, reads=["QT", "KVB", f"P{pb}"],
                      writes=[f"P{pb}"], sem="pe", inc=1)
                xsl = X[:, lt * H + dsl * 512:lt * H + dsl * 512 + 512]
                pr.op("vector", lambda v, p=P[pb], xsl=xsl, lt=lt:
                      v.scalar_tensor_tensor(
                          out=xsl, in0=p[:, :], scalar=ZI[:, lt:lt + 1],
                          in1=xsl, op0=ALU.mult, op1=ALU.add),
                      reads=[f"P{pb}", f"X{lt}", "ZI"], writes=[f"X{lt}"],
                      sem="dve", inc=1)
            if post_ln_q0:
                if lt < 8:
                    ln_stats(lt)
                if 3 <= lt < 11:
                    ln_apply(lt - 3)

    n_st = len(stages)
    for i, st in enumerate(stages):
        if st == "f":
            pr.alias("C_h", ["QT"])
            pr.alias("A_g", ["KVS", "KVB"])
            pr.alias("A_u", ["QPF", "KPF", "KPB", "LN2B", "LN2T", "ZS",
                     "QPF2", "KPF2", "KPB2", "LN2B2", "LN2T2"])
            pr.alias("B_d", ["QPJ", "KPJ"])
            ffn_call(skip_ln_q0=(i > 0 and stages[i - 1] == "a"),
                     store_out=(i == n_st - 1),
                     tail_stats=(i + 1 < n_st and stages[i + 1] == "a"))
        elif st == "a":
            attention(stats_prefilled=(i > 0 and stages[i - 1] == "f"),
                      post_ln_q0=(i + 1 < n_st and stages[i + 1] == "f"))

    if not (stages and stages[-1] == "f"):
        pr.dma_group("sync", "sp", [(lambda s: s.dma_start(
            out=x_out3, in_=X[:, :].rearrange("p (t d) -> p t d", t=n_lt)),
            [f"X{lt}" for lt in range(n_lt)], ["x_out"])])
    pr.waitpoint("sync", ["x_out"])

    sem_handles = {}
    ctxs = []
    for s in pr.sem_names:
        cm = nc.semaphore(s)
        sem_handles[s] = cm.__enter__()
        ctxs.append(cm)
    blk = nc.Block()
    block = blk.__enter__()
    pr.replay(block, sem_handles)
    blk.__exit__(None, None, None)
    for cm in reversed(ctxs):
        cm.__exit__(None, None, None)
    return nc


def kernel(L0_attn_x, L0_attn_q_proj, L0_attn_k_proj, L0_ffn_Wg_s, L0_ffn_Wu_s,
           L0_ffn_Wd_s, L0_ffn_Wg, L0_ffn_Wu, L0_ffn_Wd, L0_ffn_Wr, L0_ffn_rb):
    x = np.ascontiguousarray(np.asarray(L0_attn_x))
    B, S, _ = x.shape
    shards = x.reshape(B * 4, S // 4, H)
    wr_t = np.ascontiguousarray(np.asarray(L0_ffn_Wr).T)
    rb_b = np.ascontiguousarray(
        np.broadcast_to(np.asarray(L0_ffn_rb), (128, E_R)))
    nc = build_nc()
    common = {
        "wg_s": np.ascontiguousarray(L0_ffn_Wg_s),
        "wu_s": np.ascontiguousarray(L0_ffn_Wu_s),
        "wd_s": np.ascontiguousarray(L0_ffn_Wd_s),
        "wg_r": np.ascontiguousarray(L0_ffn_Wg),
        "wu_r": np.ascontiguousarray(L0_ffn_Wu),
        "wd_r": np.ascontiguousarray(L0_ffn_Wd),
        "wr_t": wr_t,
        "rb_b": rb_b,
        "qproj": np.ascontiguousarray(L0_attn_q_proj),
        "kproj": np.ascontiguousarray(L0_attn_k_proj),
    }
    in_maps = [dict(common, x_in=np.ascontiguousarray(shards[c]))
               for c in range(N_CORES)]
    res = run_bass_kernel_spmd(nc, in_maps, core_ids=list(range(N_CORES)))
    out = np.stack([np.asarray(res.results[c]["x_out"]) for c in range(N_CORES)])
    return out.reshape(B, S, H).astype(np.float32)
